# version 18
# baseline (speedup 1.0000x reference)
import os
import numpy as np
import ml_dtypes
import contextlib
import concourse.bass as bass
import concourse.mybir as mybir
from concourse.bass_utils import run_bass_kernel_spmd

F32 = mybir.dt.float32
BF16 = mybir.dt.bfloat16
I32 = mybir.dt.int32
I16 = mybir.dt.int16
AF = mybir.ActivationFunctionType
ALU = mybir.AluOpType
AX = mybir.AxisListType

S = 2048
E = 1024
H = 16
D = 64
NT = S // 128
NCH = E // 128
KR = 7
KMIR = 0
NR = NT + 5
BF16NP = ml_dtypes.bfloat16
_LAST_SCHED = None


class Sched:
    def __init__(self, nc, stack):
        self.nc = nc
        names = ["sS", "sG", "sPE", "sDV", "sAC", "sPL", "sO",
                 "sC", "sW8", "sXR0", "sXR1", "sWQ", "sWO", "sRW", "sX"]
        self.sem = {n: stack.enter_context(nc.semaphore(n)) for n in names}
        self.cnt = {n: 0 for n in names}
        self.plan = {n: [] for n in
                     ["sync", "tensor", "vector", "scalar", "gpsimd"]}
        self.trace = {n: [] for n in
                      ["sync", "tensor", "vector", "scalar", "gpsimd"]}

    AUTO_EDGE = {"vector": "sDV", "scalar": "sAC", "gpsimd": "sPL"}

    def W(self, eng, sem, val):
        if val and val > 0:
            h = self.sem[sem]
            self.trace[eng].append(("w", sem, val))
            self.plan[eng].append(lambda e, h=h, v=val: e.wait_ge(h, v))

    def _self_edge(self, eng):
        sem = self.AUTO_EDGE.get(eng)
        if sem is not None and self.cnt[sem] > 0:
            self.W(eng, sem, self.cnt[sem])

    def OP(self, eng, fn):
        sem = self.AUTO_EDGE.get(eng)
        if sem is not None:
            self.INC(eng, sem, fn)
        else:
            self.trace[eng].append(("op",))
            self.plan[eng].append(fn)

    def INC(self, eng, sem, fn, k=1):
        self._self_edge(eng)
        self.cnt[sem] += k
        h = self.sem[sem]
        self.trace[eng].append(("i", sem, k))
        self.plan[eng].append(
            lambda e, fn=fn, h=h, k=k: fn(e).then_inc(h, k))
        return self.cnt[sem]


class _Row:

    def __init__(self, base, off, n):
        self.base, self.off, self.n = base, off, n

    def ap(self):
        return self.base.ap()[0:1, self.off:self.off + self.n]


def _mask4_np():
    i = np.arange(128)[:, None]
    c = np.arange(640)[None, :]
    return ((c >= i + 1) & (c <= i + 512)).astype(np.float32).astype(BF16NP)


def build_nc():
    nc = bass.Bass("TRN2", target_bir_lowering=False)
    d = {}
    d["x"] = nc.dram_tensor("x", [S, E], F32, kind="ExternalInput").ap()
    d["wqkv"] = nc.dram_tensor("wqkv", [E, 3 * E], F32, kind="ExternalInput").ap()
    d["bqkv"] = nc.dram_tensor("bqkv", [1, 3 * E], F32, kind="ExternalInput").ap()
    for n in ["gq", "bq", "gk", "bk"]:
        d[n] = nc.dram_tensor(n, [1, D], F32, kind="ExternalInput").ap()
    d["wout"] = nc.dram_tensor("wout", [E, E], F32, kind="ExternalInput").ap()
    d["bout"] = nc.dram_tensor("bout", [1, E], F32, kind="ExternalInput").ap()
    d["w8"] = nc.dram_tensor("w8", [E, E], I32, kind="ExternalInput").ap()
    d["ws"] = nc.dram_tensor("ws", [1, 1], F32, kind="ExternalInput").ap()
    d["fb"] = nc.dram_tensor("fb", [1, E], F32, kind="ExternalInput").ap()
    d["mask4"] = nc.dram_tensor("mask4", [128, 640], BF16, kind="ExternalInput").ap()
    d["ident"] = nc.dram_tensor("ident", [128, 128], BF16, kind="ExternalInput").ap()
    d_out = nc.dram_tensor("out", [S, E], F32, kind="ExternalOutput").ap()

    st = contextlib.ExitStack()
    sb = lambda n, s, dt_: st.enter_context(nc.sbuf_tensor(n, s, dt_))
    psa = lambda n, s, dt_: st.enter_context(nc.psum_tensor(n, s, dt_))
    with st:
        wqkv = sb("wqkvb", [128, NCH * 3 * E], BF16)
        wout = sb("woutb", [128, NCH * E], BF16)
        wfT = sb("wfT", [128, NCH * E], BF16)
        xbf = [sb("xbf0", [128, E], BF16)] * 2
        xfm = [sb(f"xfm{i}", [128, E], BF16) for i in range(2)]
        qfm = [sb(f"qfm{i}", [128, E], BF16) for i in range(2)]
        kfm = sb("kfm", [128, NCH * (KR + KMIR) * 128], BF16)
        vtm = sb("vtm", [128, KR * H * 65], BF16)
        n_q = sb("n_q", [128, E], BF16)
        n_k = sb("n_k", [128, E], BF16)
        sqb = sb("sqqk", [128, E], BF16)
        zzb = sb("zzb", [128, E], BF16)
        sq2 = sqb
        probs = [sb(f"probs{i}", [128, 640], BF16) for i in range(4)]
        probsT = [sb(f"probsT{i}", [128, 640], BF16) for i in range(4)]
        attn = [sb(f"attn{i}", [128, E], BF16) for i in range(2)]
        araw = sb("araw", [128, H * 65], BF16)
        afm = [sb("afm0", [128, E], BF16)] * 2
        q8tm = [sb(f"q8tm{i}", [128, E], BF16) for i in range(2)]
        q8fm = [sb("q8fm0", [128, E], BF16)] * 2
        xres = [sb(f"xres{i}", [128, E], F32) for i in range(2)]
        hb = [sb(f"hb{i}", [128, E], F32) for i in range(2)]
        x1b = [sb(f"x1b{i}", [128, E], F32) for i in range(2)]
        ry = sb("ry", [128, E], F32)
        rci = sb("rci", [128, E], I16)
        re0 = sb("re0", [128, E], BF16)
        re1 = sb("re1", [128, E], BF16)
        fbr = _Row(re0, 0, E)
        growb = _Row(re1, 0, 4 * D)
        grow = _Row(ry, 0, 4 * D)
        ones32 = _Row(ry, 512, 128)
        bqkvr = sb("bqkvr", [1, 3 * E], BF16)
        boutr = sb("boutr", [1, E], BF16)
        fbrep = sb("fbrep", [128, E], BF16)
        reps = {n: sb(n + "rep", [128, E], BF16)
                for n in ["gq", "bq", "gk", "bk"]}
        mask4 = sb("mask4s", [128, 640], BF16)
        ident = sb("idents", [128, 128], BF16)
        onesb = sb("onesb", [1, 128], BF16)
        wss = sb("wss", [1, 1], F32)
        cst = sb("cst", [128, 2], F32)
        fence = sb("fence", [128, 2], F32)
        wsr = sb("wsr", [128, 1], F32)
        ar = [sb(f"arena{i}", [128, 256], F32) for i in range(2)]
        lar = [sb(f"lar{i}", [128, 16], F32) for i in range(2)]
        negC = [sb(f"negC{i}", [128, 16], F32) for i in range(2)]
        rcp = [sb(f"rcp{i}", [128, 16], F32) for i in range(2)]

        big0 = psa("big0", [128, 1024], F32)
        big1 = psa("big1", [128, 1024], F32)
        scp = psa("scp", [128, 1024], F32)
        ptp = psa("ptp", [128, 1024], BF16)
        avp = psa("avp", [128, 512], F32)
        ptb = ptp.ap()

        S_ = Sched(nc, st)
        global _LAST_SCHED
        _LAST_SCHED = S_
        W, OP, INC = S_.W, S_.OP, S_.INC
        G = {"ptev": ("sDV", 0), "sqr": 0, "sqf": 0, "rysq": 0, "rsv": 0,
             "big0": 0, "big0k": "sDV", "big1": 0, "big1k": "sDV"}

        def FDV():
            return INC("vector", "sDV", lambda e: e.tensor_copy(
                out=fence.ap()[:, 0:1], in_=cst.ap()[:, 0:1]))

        def FAC():
            return INC("scalar", "sAC", lambda e: e.activation(
                out=fence.ap()[:, 1:2], in_=cst.ap()[:, 0:1], func=AF.Square))

        def transpose_group(src_ap_fn, nblk, wait_list, tag):
            for sem, val in wait_list:
                W("tensor", sem, val)
            W("tensor", G["ptev"][0], G["ptev"][1])
            vPE = None
            for c in range(nblk):
                src = src_ap_fn(c)
                fn = lambda e, src=src, c=c: e.transpose(
                    ptb[:, c * 128:(c + 1) * 128], src, ident.ap())
                if c == nblk - 1:
                    vPE = INC("tensor", "sPE", fn)
                else:
                    OP("tensor", fn)
            return vPE

        def pt_evac(dst_ap, w, vPE):
            W("vector", "sPE", vPE)
            v = INC("vector", "sDV", lambda e, a=dst_ap, w=w: e.tensor_copy(
                out=a, in_=ptb[:, 0:w]))
            G["ptev"] = ("sDV", v)
            return v

        vS_const = 0
        for dst, src in [(mask4.ap(), d["mask4"]), (ident.ap(), d["ident"]),
                         (grow.ap()[:, 0:D], d["gq"]),
                         (grow.ap()[:, D:2 * D], d["bq"]),
                         (grow.ap()[:, 2 * D:3 * D], d["gk"]),
                         (grow.ap()[:, 3 * D:4 * D], d["bk"]),
                         (wss.ap(), d["ws"])]:
            vS_const = INC("sync", "sC",
                           lambda e, a=dst, b=src: e.dma_start(out=a, in_=b), 16)
        vG_wqkv = INC("gpsimd", "sWQ", lambda e: e.dma_start(
            out=wqkv.ap().rearrange("p (k n) -> p k n", n=3 * E),
            in_=d["wqkv"].rearrange("(k p) n -> p k n", p=128)), 16)
        vG_wout = INC("gpsimd", "sWO", lambda e: e.dma_start(
            out=wout.ap().rearrange("p (k n) -> p k n", n=E),
            in_=d["wout"].rearrange("(k p) n -> p k n", p=128)), 16)
        vG_rows = 0
        for dst, src in [(bqkvr.ap(), d["bqkv"]), (boutr.ap(), d["bout"]),
                         (fbr.ap(), d["fb"])]:
            vG_rows = INC("gpsimd", "sRW",
                          lambda e, a=dst, b=src: e.dma_start(out=a, in_=b), 16)
        vG_x = {}
        vG_x[0] = INC("gpsimd", "sX", lambda e: e.dma_start(
            out=xbf[0].ap(), in_=d["x"][0:128, :]), 16)

        OP("vector", lambda e: e.memset(cst.ap()[:, 0:1], 1e-5))
        OP("vector", lambda e: e.memset(cst.ap()[:, 1:2], 1e-6))
        OP("vector", lambda e: e.memset(onesb.ap(), 1.0))
        OP("vector", lambda e: e.memset(ones32.ap(), 1.0))
        vDV_vones = INC("vector", "sDV", lambda e: e.memset(
            vtm.ap().rearrange("p (s h c) -> p s h c", h=H, c=65)
            [:, :, :, 64:65], 1.0))
        W("vector", "sC", vS_const)
        OP("vector", lambda e: e.tensor_single_scalar(
            out=grow.ap()[:, 0:2 * D], in_=grow.ap()[:, 0:2 * D],
            scalar=0.125, op=ALU.mult))
        vDV_growb = INC("vector", "sDV",
                        lambda e: e.tensor_copy(out=growb.ap(), in_=grow.ap()))

        W("tensor", "sDV", vDV_growb)
        vDV_reps = 0
        for idx, name in enumerate(["gq", "bq", "gk", "bk"]):
            rep = reps[name]
            vPE = INC("tensor", "sPE", lambda e, idx=idx: e.matmul(
                big0.ap()[:, 0:64], onesb.ap()[0:1, 0:128],
                growb.ap()[0:1, idx * D:(idx + 1) * D], start=True, stop=True))
            W("vector", "sPE", vPE)
            vDV = INC("vector", "sDV", lambda e, rep=rep: e.tensor_copy(
                out=rep.ap()[:, 0:64], in_=big0.ap()[:, 0:64]))
            W("tensor", "sDV", vDV)
            for s_ in range(1, 16):
                fn = lambda e, rep=rep, s_=s_: e.tensor_copy(
                    out=rep.ap()[:, s_ * 64:(s_ + 1) * 64],
                    in_=rep.ap()[:, 0:64])
                if s_ == 15 and name == "bk":
                    vDV_reps = INC("vector", "sDV", fn)
                else:
                    OP("vector", fn)

        vPE_ws = INC("tensor", "sPE", lambda e: e.matmul(
            big0.ap()[:, 64:65], ones32.ap()[0:1, 0:128], wss.ap(),
            start=True, stop=True))
        W("vector", "sPE", vPE_ws)
        vDV_wsr = INC("vector", "sDV", lambda e: e.tensor_copy(
            out=wsr.ap(), in_=big0.ap()[:, 64:65]))

        W("tensor", "sRW", vG_rows)
        W("tensor", "sDV", vDV_wsr)
        vPE_fb = 0
        for n in range(2):
            vPE_fb = INC("tensor", "sPE", lambda e, n=n: e.matmul(
                big0.ap()[:, n * 512:(n + 1) * 512], onesb.ap()[0:1, 0:128],
                fbr.ap()[0:1, n * 512:(n + 1) * 512], start=True, stop=True))
        W("vector", "sPE", vPE_fb)
        vDV_fbrep = INC("vector", "sDV", lambda e: e.tensor_copy(
            out=fbrep.ap(), in_=big0.ap()))
        G["big0"], G["big0k"] = vDV_fbrep, "sDV"

        vPE_wtr = 0
        vDV_cast = 0
        W("tensor", "sC", vS_const)
        for k in range(NCH):
            vDV_cast_h = 0
            for hf in range(2):
                W("sync", "sDV", vDV_cast if hf == 0 else vDV_cast_h)
                vS_w8 = INC("sync", "sW8", lambda e, k=k, hf=hf: e.dma_start(
                    out=rci.ap().bitcast(I32),
                    in_=d["w8"][k * 128:(k + 1) * 128,
                                hf * 512:(hf + 1) * 512]), 16)
                W("vector", "sW8", vS_w8)
                if hf == 0:
                    W("vector", "sPE", vPE_wtr)
                vDV_cast_h = INC("vector", "sDV", lambda e, hf=hf: e.tensor_copy(
                    out=q8tm[0].ap()[:, hf * 512:(hf + 1) * 512],
                    in_=rci.ap().bitcast(I32)))
            vDV_cast = vDV_cast_h
            vPE_wtr = transpose_group(
                lambda c: q8tm[0].ap()[:, c * 128:(c + 1) * 128], NCH,
                [("sDV", vDV_cast)], "w8")
            dst = wfT.ap().rearrange("p (c n) -> p c n", n=E)[
                :, :, k * 128:(k + 1) * 128]
            W("vector", "sPE", vPE_wtr)
            _v = INC("vector", "sDV", lambda e, dst=dst: e.tensor_copy(
                out=dst, in_=ptb.rearrange("p (c n) -> p c n", n=128)))
            G["ptev"] = ("sDV", _v)

        vDV_xfm, vDV_qfm, vDV_kfm, vDV_vev = {}, {}, {}, {}
        vPE_qkv_q, vPE_qkv_k, vPE_qkv_v = {}, {}, {}
        vAC_attn_done, vAC_x1, vDV_q8, vDV_fsc = {}, {}, {}, {}
        vS_xres, vPE_xtr_d, vAC_out_d, vPL_h1_d, sO_d = {}, {}, {}, {}, {}
        vDV_h0_d, vAC_y0_d = {}, {}
        vPE_qtr_d, vPE_ktr_d, vDV_nq, vDV_nk = {}, {}, {}, {}
        last_exp, last_avev, vPE_tr_u, vDV_pt_u, vPE_av_u = {}, {}, {}, {}, {}
        expPEgate = {}

        def x_transposes(tt):
            vPE = transpose_group(
                lambda c, tt=tt: xbf[tt % 2].ap()[:, c * 128:(c + 1) * 128],
                NCH, [("sX", vG_x[tt])], "x")
            vPE_xtr_d[tt] = vPE
            vDV_xfm[tt] = pt_evac(xfm[tt % 2].ap(), 1024, vPE)

        def qkv_wave(t, n0):
            xf = xfm[t % 2].ap()
            for kk in range(NCH):
                lhs = xf[:, kk * 128:(kk + 1) * 128]
                for j in range(2):
                    n = n0 + j
                    rhs = wqkv.ap()[:, kk * 3 * E + n * 512:
                                    kk * 3 * E + (n + 1) * 512]
                    ps = big1.ap()[:, j * 512:(j + 1) * 512]
                    OP("tensor", lambda e, l=lhs, r=rhs, ps=ps, kk=kk: e.matmul(
                        ps, l, r, start=(kk == 0), stop=False))
            v = None
            for j in range(2):
                n = n0 + j
                rb = bqkvr.ap()[0:1, n * 512:(n + 1) * 512]
                ps = big1.ap()[:, j * 512:(j + 1) * 512]
                v = INC("tensor", "sPE", lambda e, ps=ps, rb=rb: e.matmul(
                    ps, onesb.ap()[0:1, 0:128], rb, start=False, stop=True))
            return v

        def seg_stats(t, half, bigap, vPE_mm, eps):
            A = ar[t % 2].ap()
            o = 16 * half
            su, ss = A[:, 0 + o:16 + o], A[:, 32 + o:48 + o]
            mu, ms = A[:, 64 + o:80 + o], A[:, 96 + o:112 + o]
            vr, rs = A[:, 128 + o:144 + o], A[:, 160 + o:176 + o]
            W("vector", "sPE", vPE_mm)
            vDV_cp = INC("vector", "sDV", lambda e, b=bigap: e.tensor_copy(
                out=zzb.ap(), in_=b))
            INC("vector", "sDV", lambda e, su=su: e.tensor_reduce(
                out=su, in_=zzb.ap().rearrange("p (s dd) -> p s dd", dd=64),
                axis=AX.X, op=ALU.add))
            W("scalar", "sDV", vDV_cp)
            W("scalar", "sDV", G["sqr"])
            vAC_sq = INC("scalar", "sAC", lambda e: e.activation(
                out=sqb.ap(), in_=zzb.ap(), func=AF.Square))
            W("vector", "sAC", vAC_sq)
            G["sqr"] = INC("vector", "sDV", lambda e, ss=ss: e.tensor_reduce(
                out=ss, in_=sqb.ap().rearrange("p (s dd) -> p s dd", dd=64),
                axis=AX.X, op=ALU.add))
            OP("vector", lambda e, mu=mu, su=su: e.tensor_single_scalar(
                out=mu, in_=su, scalar=1.0 / 64, op=ALU.mult))
            OP("vector", lambda e, ms=ms, ss=ss: e.tensor_single_scalar(
                out=ms, in_=ss, scalar=1.0 / 64, op=ALU.mult))
            OP("vector", lambda e, vr=vr, mu=mu: e.tensor_mul(vr, mu, mu))
            INC("vector", "sDV",
                lambda e, vr=vr, ms=ms: e.tensor_sub(vr, ms, vr))
            vDV_var = FDV()
            W("scalar", "sDV", vDV_var)
            INC("scalar", "sAC", lambda e, vr=vr: e.activation(
                out=vr, in_=vr, func=AF.Ln, bias=cst.ap()[:, 0:1]))
            INC("scalar", "sAC", lambda e, rs=rs, vr=vr: e.activation(
                out=rs, in_=vr, func=AF.Exp, scale=-0.5))
            vAC_rs = FAC()
            return mu, rs, vAC_rs, vDV_cp

        def rstd_chain(L, c_in, c_out, epscol):
            W("scalar", "sDV", G["lnch"])
            INC("scalar", "sAC", lambda e, a=L[:, c_in:c_in + 1]:
                e.activation(out=a, in_=a, func=AF.Ln,
                             bias=cst.ap()[:, epscol:epscol + 1]))
            INC("scalar", "sAC",
                lambda e, o=L[:, c_out:c_out + 1],
                a=L[:, c_in:c_in + 1]: e.activation(out=o, in_=a,
                                                    func=AF.Exp, scale=-0.5))
            return FAC()

        G["lnch"] = 0

        def attn_geom(aa):
            nblk = min(aa + 1, 5)
            w = 128 * nblk
            kt0 = max(0, aa - 4)
            wrapped = aa >= KR and (aa % KR) < 4
            kcol0 = (kt0 % KR) * 128 if not wrapped else None
            return nblk, w, kt0, kcol0

        def stage_scores(aa, h):
            u = aa * H + h
            nblk, w, kt0, kcol0 = attn_geom(aa)
            scb = big0 if u % 2 == 0 else scp
            rows0 = 64 * (h % 2)
            ch = h // 2
            lhs = qfm[aa % 2].ap()[rows0:rows0 + 64, ch * 128:(ch + 1) * 128]
            kb = kfm.ap()[rows0:rows0 + 64,
                          ch * (KR + KMIR) * 128:(ch + 1) * (KR + KMIR) * 128]
            W("tensor", "sAC", last_exp.get(u - 2, 0))
            if kcol0 is not None:
                n1 = min(w, 512)
                r1 = kb[:, kcol0:kcol0 + n1]
                vPE_sc = INC("tensor", "sPE", lambda e, l=lhs, r=r1, n1=n1,
                             scb=scb: e.matmul(scb.ap()[:, 0:n1], l, r,
                                               start=True, stop=True))
                if w > 512:
                    r2 = kb[:, kcol0 + 512:kcol0 + w]
                    vPE_sc = INC("tensor", "sPE", lambda e, l=lhs, r=r2, w=w,
                                 scb=scb: e.matmul(scb.ap()[:, 512:w], l, r,
                                                   start=True, stop=True))
            else:
                vPE_sc = None
                for j in range(nblk):
                    kc = ((kt0 + j) % KR) * 128
                    rj = kb[:, kc:kc + 128]
                    fn = lambda e, l=lhs, r=rj, j=j, scb=scb: e.matmul(
                        scb.ap()[:, j * 128:(j + 1) * 128], l, r,
                        start=True, stop=True)
                    if j == nblk - 1:
                        vPE_sc = INC("tensor", "sPE", fn)
                    else:
                        OP("tensor", fn)
            cst = 256 if aa >= 4 else max(0, 128 * aa - 63)
            cw = 64 if aa >= 4 else min(64, 128 * aa + 1 - cst)
            W("vector", "sPE", vPE_sc)
            dstC = negC[aa % 2].ap()[:, h:h + 1]
            INC("vector", "sDV", lambda e, dstC=dstC, cst=cst, cw=cw, scb=scb:
                e.tensor_reduce(out=dstC, in_=scb.ap()[:, cst:cst + cw],
                                axis=AX.X, op=ALU.max, negate=True))
            vDV_mx = FDV()
            W("scalar", "sDV", vDV_mx)
            W("scalar", "sPE", expPEgate.get(u, 0))
            pb = probs[u % 4].ap()
            last_exp[u] = INC("scalar", "sAC", lambda e, pb=pb, w=w, dstC=dstC,
                              scb=scb: e.activation(
                out=pb[:, 0:w], in_=scb.ap()[:, 0:w],
                func=AF.Exp, bias=dstC, scale=1.0))
            moff = 0 if aa >= 4 else 512 - 128 * aa
            W("gpsimd", "sAC", last_exp[u])
            return INC("gpsimd", "sPL", lambda e, pb=pb, w=w, moff=moff:
                       e.tensor_mul(pb[:, 0:w], pb[:, 0:w],
                                    mask4.ap()[:, moff:moff + w]))

        def stage_transpose(aa, h, vPL_m):
            u = aa * H + h
            nblk, w, _, _ = attn_geom(aa)
            pb = probs[u % 4].ap()
            vPE = transpose_group(
                lambda c, pb=pb: pb[:, c * 128:(c + 1) * 128],
                nblk, [("sPL", vPL_m)], "pT")
            vPE_tr_u[u] = vPE
            dst = probsT[u % 4].ap()[:, 0:w]
            if u % 2 == 1:
                W("scalar", "sPE", vPE_av_u.get(u - 4, 0))
                W("scalar", "sPE", vPE)
                INC("scalar", "sAC", lambda e, dst=dst, w=w: e.activation(
                    out=dst, in_=ptb[:, 0:w], func=AF.Copy))
                v = FAC()
                G["ptev"] = ("sAC", v)
                vDV_pt_u[u] = ("sAC", v)
            else:
                W("vector", "sPE", vPE_av_u.get(u - 4, 0))
                vDV_pt_u[u] = ("sDV", pt_evac(dst, w, vPE))
            expPEgate[u + 4] = vPE

        def stage_av(aa, h):
            u = aa * H + h
            nblk, w, kt0, _ = attn_geom(aa)
            W("tensor", "sAC", last_avev.get(u - 1, 0))
            W("tensor", vDV_pt_u[u][0], vDV_pt_u[u][1])
            pT = probsT[u % 4].ap()
            vPE_av = None
            for j in range(nblk):
                kt = kt0 + j
                vb = (kt % KR) * H * 65 + h * 65
                fn = lambda e, pT=pT, j=j, vb=vb, last=(j == nblk - 1): e.matmul(
                    avp.ap()[:, 0:65], pT[:, j * 128:(j + 1) * 128],
                    vtm.ap()[:, vb:vb + 65], start=(j == 0), stop=last)
                if j == nblk - 1:
                    vPE_av = INC("tensor", "sPE", fn)
                else:
                    OP("tensor", fn)
            vPE_av_u[u] = vPE_av
            W("scalar", "sPE", vPE_av)
            if h == 0:
                W("scalar", "sDV", G.get("anorm", 0))
            dstA = araw.ap()[:, h * 65:(h + 1) * 65]
            last_avev[u] = INC("scalar", "sAC", lambda e, dstA=dstA:
                               e.activation(out=dstA, in_=avp.ap()[:, 0:65],
                                            func=AF.Copy))

        def attn_half(aa, half):
            if half == 0:
                W("tensor", "sDV", vDV_qfm[aa])
                W("tensor", "sDV", vDV_kfm[aa])
                W("tensor", "sDV", vDV_vev[aa])
                G["attn_pend"] = []
            pend = G["attn_pend"]
            for h in range(8 * half, 8 * half + 8):
                m = stage_scores(aa, h)
                pend.append((h, m))
                if len(pend) >= 3:
                    hh, mm = pend.pop(0)
                    stage_transpose(aa, hh, mm)
                    if hh >= 2:
                        stage_av(aa, hh - 2)
                drain_chain(1)
            if half == 1:
                for hh, mm in pend:
                    stage_transpose(aa, hh, mm)
                for hh in range(H - 4, H):
                    stage_av(aa, hh)
                W("vector", "sAC", last_avev[aa * H + H - 1])
                a3 = araw.ap().rearrange("p (h c) -> p h c", c=65)
                INC("vector", "sDV", lambda e, a3=a3: e.reciprocal(
                    out=rcp[aa % 2].ap()[:, 0:16], in_=a3[:, :, 64:65]))
                rb_ = rcp[aa % 2].ap().rearrange(
                    "p (h u) -> p h u", u=1).broadcast_to([128, 16, 64])
                vno = INC("vector", "sDV", lambda e, a3=a3, rb_=rb_, aa=aa:
                          e.tensor_tensor(
                    out=attn[aa % 2].ap().rearrange("p (h c) -> p h c", c=64),
                    in0=a3[:, :, 0:64], in1=rb_, op=ALU.mult))
                G["anorm"] = vno
                vAC_attn_done[aa] = vno

        pending_chain = []

        def drain_chain(n=None):
            k = len(pending_chain) if n is None else min(n, len(pending_chain))
            for _ in range(k):
                pending_chain.pop(0)()

        def ln_stats_thunks(src_sb, L, epscol, pre_waits, x1b_gates):
            def t1():
                for eng, sem, val in pre_waits:
                    W(eng, sem, val())
                W("scalar", "sDV", G["sqf"])
                G["rysq"] = INC("scalar", "sAC", lambda e: e.activation(
                    out=ry.ap(), in_=src_sb.ap(), func=AF.Square))
                for fn in x1b_gates:
                    fn()
                INC("vector", "sDV", lambda e: e.tensor_reduce(
                    out=L[:, 0:1], in_=src_sb.ap(), axis=AX.X, op=ALU.add))

            def t2():
                W("vector", "sAC", G["rysq"])
                G["sqf"] = INC("vector", "sDV", lambda e: e.tensor_reduce(
                    out=L[:, 1:2], in_=ry.ap(), axis=AX.X, op=ALU.add))
                OP("vector", lambda e: e.tensor_single_scalar(
                    out=L[:, 2:3], in_=L[:, 0:1], scalar=1.0 / E, op=ALU.mult))
                OP("vector", lambda e: e.tensor_single_scalar(
                    out=L[:, 3:4], in_=L[:, 1:2], scalar=1.0 / E, op=ALU.mult))
                OP("vector", lambda e: e.tensor_mul(
                    L[:, 4:5], L[:, 2:3], L[:, 2:3]))
                INC("vector", "sDV", lambda e: e.tensor_sub(
                    L[:, 4:5], L[:, 3:4], L[:, 4:5]))
                G["lnch"] = FDV()
                G["rsv"] = rstd_chain(L, 4, 5, epscol)
            return [t1, t2]

        def make_pc_chain(p):
            L = lar[p % 2].ap()
            src = hb[p % 2]
            th = ln_stats_thunks(
                src, L, 0,
                pre_waits=[("scalar", "sDV", lambda p=p: vDV_h0_d[p])],
                x1b_gates=[
                    lambda p=p: W("vector", "sDV", vPL_h1_d.get(p - 2, 0)),
                    lambda p=p: W("vector", "sO", sO_d.get(p - 2, 0))])

            def t3():
                W("vector", "sAC", G["rsv"])
                vAC_x1[p] = INC("vector", "sDV", lambda e: e.tensor_scalar(
                    out=x1b[p % 2].ap(), in0=src.ap(), scalar1=L[:, 2:3],
                    scalar2=L[:, 5:6], op0=ALU.subtract, op1=ALU.mult))

            def t4():
                W("scalar", "sDV", vAC_x1[p])
                W("scalar", "sDV", G["sqf"])
                G["rysq"] = INC("scalar", "sAC", lambda e: e.activation(
                    out=ry.ap(), in_=x1b[p % 2].ap(), func=AF.Square))
                W("vector", "sAC", G["rysq"])
                G["sqf"] = INC("vector", "sDV", lambda e: e.tensor_reduce(
                    out=L[:, 12:13], in_=ry.ap(), axis=AX.X, op=ALU.add))
                OP("vector", lambda e: e.tensor_reduce(
                    out=L[:, 7:8], in_=x1b[p % 2].ap(), axis=AX.X, op=ALU.max,
                    apply_absolute_value=True))
                INC("vector", "sDV", lambda e: e.tensor_single_scalar(
                    out=L[:, 12:13], in_=L[:, 12:13], scalar=1.0 / E,
                    op=ALU.mult))
                G["lnch"] = FDV()
                G["rsv"] = rstd_chain(L, 12, 13, 1)

            def t5():
                W("vector", "sAC", G["rsv"])
                OP("vector", lambda e: e.tensor_mul(
                    L[:, 8:9], L[:, 7:8], L[:, 13:14]))
                OP("vector", lambda e: e.tensor_single_scalar(
                    out=L[:, 8:9], in_=L[:, 8:9], scalar=1e-5, op=ALU.max))
                OP("vector", lambda e: e.reciprocal(
                    out=L[:, 9:10], in_=L[:, 8:9]))
                OP("vector", lambda e: e.tensor_mul(
                    L[:, 10:11], L[:, 13:14], L[:, 9:10]))
                OP("vector", lambda e: e.tensor_single_scalar(
                    out=L[:, 10:11], in_=L[:, 10:11], scalar=127.0,
                    op=ALU.mult))
                W("vector", "sDV", vDV_wsr)
                OP("vector", lambda e: e.tensor_mul(
                    L[:, 11:12], L[:, 8:9], wsr.ap()))
                INC("vector", "sDV", lambda e: e.tensor_single_scalar(
                    out=L[:, 11:12], in_=L[:, 11:12], scalar=1.0 / 127,
                    op=ALU.mult))
                vDV_fsc[p] = FDV()

            def t6():
                OP("vector", lambda e: e.tensor_scalar(
                    out=ry.ap(), in0=x1b[p % 2].ap(), scalar1=L[:, 10:11],
                    scalar2=256.5, op0=ALU.mult, op1=ALU.add))
                OP("vector", lambda e: e.tensor_copy(
                    out=rci.ap(), in_=ry.ap()))
                G["sqf"] = INC("vector", "sDV", lambda e: e.tensor_tensor(
                    out=ry.ap(), in0=ry.ap(), in1=rci.ap(), op=ALU.subtract))

            def t7():
                OP("vector", lambda e: e.tensor_single_scalar(
                    out=re1.ap(), in_=ry.ap(), scalar=1.0, op=ALU.is_ge))
                OP("vector", lambda e: e.tensor_single_scalar(
                    out=re0.ap(), in_=ry.ap(), scalar=0.0, op=ALU.is_ge))
                OP("vector", lambda e: e.scalar_tensor_tensor(
                    out=ry.ap(), in0=rci.ap(), scalar=257.0, in1=re1.ap(),
                    op0=ALU.subtract, op1=ALU.add))
                G["sqf"] = vDV_q8[p] = INC(
                    "vector", "sDV", lambda e: e.tensor_tensor(
                        out=q8tm[p % 2].ap(), in0=ry.ap(), in1=re0.ap(),
                        op=ALU.add))
            return th + [t3, t4, t5, t6, t7]

        def make_fc_chain(f):
            L2 = lar[f % 2].ap()

            def ta():
                W("vector", "sAC", vAC_y0_d[f])
                OP("vector", lambda e: e.tensor_add(
                    hb[f % 2].ap(), hb[f % 2].ap(), fbrep.ap()))
                vPL_h1_d[f] = INC("vector", "sDV", lambda e: e.tensor_add(
                    hb[f % 2].ap(), hb[f % 2].ap(), x1b[f % 2].ap()))
            th = ln_stats_thunks(
                hb[f % 2], L2, 0,
                pre_waits=[("scalar", "sDV", lambda f=f: vPL_h1_d[f])],
                x1b_gates=[])

            def tb():
                W("vector", "sAC", G["rsv"])
                vAC_o = INC("vector", "sDV", lambda e: e.tensor_scalar(
                    out=x1b[f % 2].ap(), in0=hb[f % 2].ap(),
                    scalar1=L2[:, 2:3], scalar2=L2[:, 5:6],
                    op0=ALU.subtract, op1=ALU.mult))
                vAC_out_d[f] = vAC_o
                W("sync", "sDV", vAC_o)
                sO_d[f] = INC("sync", "sO", lambda e: e.dma_start(
                    out=d_out[f * 128:(f + 1) * 128, :],
                    in_=x1b[f % 2].ap()), 16)
            return [ta] + th + [tb]

        _dbg_rounds = int(os.environ.get("ATH_DEBUG_ROUNDS", NR))
        _skip = set(os.environ.get("ATH_SKIP", "").split(","))
        for r in range(_dbg_rounds):
            t, a, p, f = r, r - 2, r - 3, r - 4
            if "qkv" in _skip:
                t = NT
            if "attn" in _skip:
                a = -1
            if "proj" in _skip:
                p = -1
            if "ffn" in _skip:
                f = -1

            pc = r - 4 if "proj" not in _skip else -1
            fc = r - 5 if "ffn" not in _skip else -1
            if 0 <= pc < NT:
                pending_chain.extend(make_pc_chain(pc))
            if 0 <= fc < NT:
                pending_chain.extend(make_fc_chain(fc))

            if t == 0:
                x_transposes(0)
            if 1 <= t + 1 < NT:
                tt = t + 1
                W("gpsimd", "sPE", vPE_xtr_d[tt - 1])
                vG_x[tt] = INC("gpsimd", "sX", lambda e, tt=tt: e.dma_start(
                    out=xbf[0].ap(),
                    in_=d["x"][tt * 128:(tt + 1) * 128, :]), 16)
                x_transposes(tt)

            if t < NT:
                W("tensor", "sWQ", vG_wqkv)
                W("tensor", "sRW", vG_rows)
                W("tensor", "sDV", vDV_xfm[t])
                W("tensor", G["big1k"], G["big1"])
                W("tensor", "sDV", vDV_vones)
                vPE_qkv_v[t] = qkv_wave(t, 4)
                W("vector", "sPE", vPE_qkv_v[t])
                if t >= KR:
                    W("vector", "sDV", vAC_attn_done[t - (KR - 4)])
                slot = t % KR
                vv = None
                for h8 in range(2):
                    src_ = big1.ap()[:, h8 * 512:(h8 + 1) * 512].rearrange(
                        "p (h c) -> p h c", c=64)
                    base = slot * H * 65 + h8 * 8 * 65
                    dst = vtm.ap()[:, base:base + 8 * 65].rearrange(
                        "p (h c) -> p h c", c=65)[:, :, 0:64]
                    fn = lambda e, s=src_, dd=dst: e.tensor_copy(out=dd, in_=s)
                    if h8 == 1:
                        vv = INC("vector", "sDV", fn)
                    else:
                        OP("vector", fn)
                vDV_vev[t] = vv
                G["big1"], G["big1k"] = vv, "sDV"

            if t + 1 < NT and t >= 0:
                pass

            if t < NT:
                W("tensor", G["big1k"], G["big1"])
                vPE_qkv_q[t] = qkv_wave(t, 0)
                mu, rs, vAC_rsq, vDV_cpq = seg_stats(t, 0, big1.ap(),
                                                     vPE_qkv_q[t], 1e-5)
                G["big1"], G["big1k"] = vDV_cpq, "sDV"
            if 0 <= t - 1 < NT:
                tm1 = t - 1
                vPE_qtr = transpose_group(
                    lambda c, tm1=tm1: n_q.ap()[:, c * 128:(c + 1) * 128],
                    NCH, [("sDV", vDV_nq[tm1])], "q")
                vPE_qtr_d[tm1] = vPE_qtr
                vDV_qfm[tm1] = pt_evac(qfm[tm1 % 2].ap(), 1024, vPE_qtr)
            if 0 <= a < NT:
                attn_half(a, 0)

            if t < NT:
                W("vector", "sAC", vAC_rsq)
                W("vector", "sPE", vPE_qtr_d.get(t - 1, 0))
                zz3 = zzb.ap().rearrange("p (s u) -> p s u", u=64)
                nq3 = n_q.ap().rearrange("p (s u) -> p s u", u=64)
                mu_b = mu.rearrange("p (s u) -> p s u", u=1).broadcast_to(
                    [128, 16, 64])
                rs_b = rs.rearrange("p (s u) -> p s u", u=1).broadcast_to(
                    [128, 16, 64])
                OP("vector", lambda e, mu_b=mu_b, zz3=zz3, nq3=nq3:
                   e.tensor_tensor(out=nq3, in0=zz3, in1=mu_b,
                                   op=ALU.subtract))
                OP("vector", lambda e, rs_b=rs_b, nq3=nq3:
                   e.tensor_tensor(out=nq3, in0=nq3, in1=rs_b, op=ALU.mult))
                W("vector", "sDV", vDV_reps)
                OP("vector", lambda e: e.tensor_mul(
                    n_q.ap(), n_q.ap(), reps["gq"].ap()))
                vDV_nq[t] = INC("vector", "sDV", lambda e: e.tensor_add(
                    n_q.ap(), n_q.ap(), reps["bq"].ap()))

            if t < NT:
                W("tensor", G["big1k"], G["big1"])
                vPE_qkv_k[t] = qkv_wave(t, 2)
                mu2, rs2, vAC_rsk, vDV_cpk = seg_stats(t, 1, big1.ap(),
                                                       vPE_qkv_k[t], 1e-5)
                G["big1"], G["big1k"] = vDV_cpk, "sDV"
            if 0 <= t - 1 < NT:
                tm1 = t - 1
                vPE_ktr = transpose_group(
                    lambda c: n_k.ap()[:, c * 128:(c + 1) * 128], NCH,
                    [("sDV", vDV_nk[tm1])], "k")
                vPE_ktr_d[tm1] = vPE_ktr
                slot = tm1 % KR
                W("vector", "sPE", vPE_ktr)
                dst = kfm.ap().rearrange(
                    "p (c w_) -> p c w_", w_=(KR + KMIR) * 128)[
                    :, :, slot * 128:(slot + 1) * 128]
                vlast = INC("vector", "sDV", lambda e, dst=dst: e.tensor_copy(
                    out=dst, in_=ptb.rearrange("p (c w_) -> p c w_", w_=128)))
                vDV_kfm[tm1] = vlast
                G["ptev"] = ("sDV", vlast)
            if 0 <= a < NT:
                attn_half(a, 1)

            if t < NT:
                W("vector", "sAC", vAC_rsk)
                W("vector", "sPE", vPE_ktr_d.get(t - 1, 0))
                zz3 = zzb.ap().rearrange("p (s u) -> p s u", u=64)
                nk3 = n_k.ap().rearrange("p (s u) -> p s u", u=64)
                mu2_b = mu2.rearrange("p (s u) -> p s u", u=1).broadcast_to(
                    [128, 16, 64])
                rs2_b = rs2.rearrange("p (s u) -> p s u", u=1).broadcast_to(
                    [128, 16, 64])
                OP("vector", lambda e, mu2_b=mu2_b, zz3=zz3, nk3=nk3:
                   e.tensor_tensor(out=nk3, in0=zz3, in1=mu2_b,
                                   op=ALU.subtract))
                OP("vector", lambda e, rs2_b=rs2_b, nk3=nk3:
                   e.tensor_tensor(out=nk3, in0=nk3, in1=rs2_b, op=ALU.mult))
                OP("vector", lambda e: e.tensor_mul(
                    n_k.ap(), n_k.ap(), reps["gk"].ap()))
                vDV_nk[t] = INC("vector", "sDV", lambda e: e.tensor_add(
                    n_k.ap(), n_k.ap(), reps["bk"].ap()))

            drain_chain()

            if 0 <= p < NT:
                vPE_atr = transpose_group(
                    lambda c, p=p: attn[p % 2].ap()[:, c * 128:(c + 1) * 128],
                    NCH, [("sDV", vAC_attn_done[p])], "a")
                vDV_afm = pt_evac(afm[p % 2].ap(), 1024, vPE_atr)
                W("tensor", G["big1k"], G["big1"])
                W("tensor", "sWO", vG_wout)
                W("tensor", "sDV", vDV_afm)
                af = afm[p % 2].ap()
                vPE_proj = None
                for n in range(2):
                    for kk in range(NCH):
                        l = af[:, kk * 128:(kk + 1) * 128]
                        rr = wout.ap()[:, kk * E + n * 512:kk * E + (n + 1) * 512]
                        OP("tensor", lambda e, l=l, rr=rr, n=n, kk=kk: e.matmul(
                            big1.ap()[:, n * 512:(n + 1) * 512], l, rr,
                            start=(kk == 0), stop=False))
                    rb = boutr.ap()[0:1, n * 512:(n + 1) * 512]
                    vPE_proj = INC("tensor", "sPE", lambda e, rb=rb, n=n: e.matmul(
                        big1.ap()[:, n * 512:(n + 1) * 512],
                        onesb.ap()[0:1, 0:128], rb, start=False, stop=True))
                W("vector", "sPE", vPE_proj)
                W("vector", "sXR%d" % (p % 2), vS_xres[p])
                W("vector", "sDV", vAC_out_d.get(p - 2, 0))
                vDV_h0 = INC("vector", "sDV", lambda e, p=p: e.tensor_add(
                    hb[p % 2].ap(), big1.ap(), xres[p % 2].ap()))
                vDV_h0_d[p] = vDV_h0
                G["big1"], G["big1k"] = vDV_h0, "sDV"

            drain_chain()

            if 0 <= f < NT:
                vPE_q8t = transpose_group(
                    lambda c, f=f: q8tm[f % 2].ap()[:, c * 128:(c + 1) * 128],
                    NCH, [("sDV", vDV_q8[f])], "q8")
                vDV_q8f = pt_evac(q8fm[f % 2].ap(), 1024, vPE_q8t)
                W("tensor", G["big1k"], G["big1"])
                W("tensor", "sDV", vDV_q8f)
                qf = q8fm[f % 2].ap()
                vPE_ffn = None
                for n in range(2):
                    for kk in range(NCH):
                        l = qf[:, kk * 128:(kk + 1) * 128]
                        rr = wfT.ap()[:, kk * E + n * 512:kk * E + (n + 1) * 512]
                        fn = lambda e, l=l, rr=rr, n=n, kk=kk: e.matmul(
                            big1.ap()[:, n * 512:(n + 1) * 512], l, rr,
                            start=(kk == 0), stop=(kk == NCH - 1))
                        if n == 1 and kk == NCH - 1:
                            vPE_ffn = INC("tensor", "sPE", fn)
                        else:
                            OP("tensor", fn)
                L2 = lar[f % 2].ap()
                W("scalar", "sPE", vPE_ffn)
                W("scalar", "sDV", vDV_fsc[f])
                W("scalar", "sDV", vAC_x1[f])
                vAC_y0 = INC("scalar", "sAC", lambda e, f=f, L2=L2:
                             e.activation(out=hb[f % 2].ap(), in_=big1.ap(),
                                          func=AF.Copy,
                                          scale=L2[:, 11:12]))
                vAC_y0_d[f] = vAC_y0
                G["big1"], G["big1k"] = vAC_y0, "sAC"

            tp = p + 1
            if 0 <= tp < NT:
                if tp >= 2:
                    W("sync", "sDV", vAC_x1[tp - 2])
                vS_xres[tp] = INC("sync", "sXR%d" % (tp % 2),
                                  lambda e, tp=tp: e.dma_start(
                    out=xres[tp % 2].ap(),
                    in_=d["x"][tp * 128:(tp + 1) * 128, :]), 16)

        if os.environ.get("ATH_DUMP"):
            dumps = [("d_nq", n_q), ("d_nk", n_k), ("d_xfm0", xfm[0]),
                     ("d_qfm0", qfm[0]), ("d_kfm", kfm), ("d_vtm", vtm),
                     ("d_attn0", attn[0]), ("d_attn1", attn[1]),
                     ("d_probs0", probs[0]), ("d_probsT0", probsT[0]),
                     ("d_x1b0", x1b[0]), ("d_hb0", hb[0]),
                     ("d_q8tm0", q8tm[0]), ("d_wfT", wfT),
                     ("d_negC0", negC[0]), ("d_rcp0", rcp[0]),
                     ("d_ar0", ar[0]), ("d_lar0", lar[0]),
                     ("d_fbrep", fbrep), ("d_gqrep", reps["gq"]),
                     ("d_wqkv", wqkv)]
            for sname in ["sPE", "sDV", "sAC", "sPL"]:
                W("sync", sname, S_.cnt[sname])
            for dn, tens in dumps:
                shape = list(tens.ap().shape)
                dt_ = tens.ap().dtype
                dram = nc.dram_tensor(dn, shape, dt_, kind="ExternalOutput").ap()
                INC("sync", "sO", lambda e, a=dram, b=tens: e.dma_start(
                    out=a, in_=b.ap()), 16)
        W("sync", "sO", S_.cnt["sO"])

        with nc.Block() as block:
            @block.sync
            def _(e):
                for fn in S_.plan["sync"]:
                    fn(e)

            @block.tensor
            def _(e):
                for fn in S_.plan["tensor"]:
                    fn(e)

            @block.vector
            def _(e):
                for fn in S_.plan["vector"]:
                    fn(e)

            @block.scalar
            def _(e):
                for fn in S_.plan["scalar"]:
                    fn(e)

            @block.gpsimd
            def _(e):
                for fn in S_.plan["gpsimd"]:
                    fn(e)

    return nc


def kernel(L0_attn_x, L0_attn_W_qkv, L0_attn_b_qkv, L0_attn_q_gamma,
           L0_attn_q_beta, L0_attn_k_gamma, L0_attn_k_beta, L0_attn_W_out,
           L0_attn_b_out, L0_ffn_w_int8, L0_ffn_weight_scale, L0_ffn_bias):
    nc = build_nc()
    x = np.ascontiguousarray(np.asarray(L0_attn_x, dtype=np.float32))
    com = {
        "wqkv": np.ascontiguousarray(np.asarray(L0_attn_W_qkv, np.float32)),
        "bqkv": np.asarray(L0_attn_b_qkv, np.float32).reshape(1, -1),
        "gq": np.asarray(L0_attn_q_gamma, np.float32).reshape(1, -1),
        "bq": np.asarray(L0_attn_q_beta, np.float32).reshape(1, -1),
        "gk": np.asarray(L0_attn_k_gamma, np.float32).reshape(1, -1),
        "bk": np.asarray(L0_attn_k_beta, np.float32).reshape(1, -1),
        "wout": np.ascontiguousarray(np.asarray(L0_attn_W_out, np.float32)),
        "bout": np.asarray(L0_attn_b_out, np.float32).reshape(1, -1),
        "w8": np.ascontiguousarray(np.asarray(L0_ffn_w_int8, np.int32)),
        "ws": np.asarray(L0_ffn_weight_scale, np.float32).reshape(1, 1),
        "fb": np.asarray(L0_ffn_bias, np.float32).reshape(1, -1),
        "mask4": _mask4_np(),
        "ident": np.eye(128, dtype=np.float32).astype(BF16NP),
    }
    in_maps = [dict(com, x=np.ascontiguousarray(x[b])) for b in range(8)]
    res = run_bass_kernel_spmd(nc, in_maps, core_ids=list(range(8)))
    return np.stack([np.asarray(r["out"], dtype=np.float32)
                     for r in res.results], axis=0)


# version 19
# speedup vs baseline: 1.0234x; 1.0234x over previous
import os
import numpy as np
import ml_dtypes
import contextlib
import concourse.bass as bass
import concourse.mybir as mybir
from concourse.bass_utils import run_bass_kernel_spmd

F32 = mybir.dt.float32
BF16 = mybir.dt.bfloat16
I32 = mybir.dt.int32
I16 = mybir.dt.int16
AF = mybir.ActivationFunctionType
ALU = mybir.AluOpType
AX = mybir.AxisListType

S = 2048
E = 1024
H = 16
D = 64
NT = S // 128
NCH = E // 128
KR = 7
KMIR = 0
NR = NT + 5
BF16NP = ml_dtypes.bfloat16
_LAST_SCHED = None


class Sched:
    def __init__(self, nc, stack):
        self.nc = nc
        names = ["sS", "sG", "sPE", "sDV", "sAC", "sPL", "sO",
                 "sC", "sW8", "sXR0", "sXR1", "sWQ", "sWO", "sRW", "sX"]
        self.sem = {n: stack.enter_context(nc.semaphore(n)) for n in names}
        self.cnt = {n: 0 for n in names}
        self.plan = {n: [] for n in
                     ["sync", "tensor", "vector", "scalar", "gpsimd"]}
        self.trace = {n: [] for n in
                      ["sync", "tensor", "vector", "scalar", "gpsimd"]}

    AUTO_EDGE = {"vector": "sDV", "scalar": "sAC", "gpsimd": "sPL"}

    def W(self, eng, sem, val):
        if val and val > 0:
            h = self.sem[sem]
            self.trace[eng].append(("w", sem, val))
            self.plan[eng].append(lambda e, h=h, v=val: e.wait_ge(h, v))

    def _self_edge(self, eng):
        sem = self.AUTO_EDGE.get(eng)
        if sem is not None and self.cnt[sem] > 0:
            self.W(eng, sem, self.cnt[sem])

    def OP(self, eng, fn):
        sem = self.AUTO_EDGE.get(eng)
        if sem is not None:
            self.INC(eng, sem, fn)
        else:
            self.trace[eng].append(("op",))
            self.plan[eng].append(fn)

    def INC(self, eng, sem, fn, k=1):
        self._self_edge(eng)
        self.cnt[sem] += k
        h = self.sem[sem]
        self.trace[eng].append(("i", sem, k))
        self.plan[eng].append(
            lambda e, fn=fn, h=h, k=k: fn(e).then_inc(h, k))
        return self.cnt[sem]


class _Row:

    def __init__(self, base, off, n):
        self.base, self.off, self.n = base, off, n

    def ap(self):
        return self.base.ap()[0:1, self.off:self.off + self.n]


def _mask4_np():
    i = np.arange(128)[:, None]
    c = np.arange(640)[None, :]
    return ((c >= i + 1) & (c <= i + 512)).astype(np.float32).astype(BF16NP)


def build_nc():
    nc = bass.Bass("TRN2", target_bir_lowering=False)
    d = {}
    d["x"] = nc.dram_tensor("x", [S, E], F32, kind="ExternalInput").ap()
    d["wqkv"] = nc.dram_tensor("wqkv", [E, 3 * E], F32, kind="ExternalInput").ap()
    d["bqkv"] = nc.dram_tensor("bqkv", [1, 3 * E], F32, kind="ExternalInput").ap()
    for n in ["gq", "bq", "gk", "bk"]:
        d[n] = nc.dram_tensor(n, [1, D], F32, kind="ExternalInput").ap()
    d["wout"] = nc.dram_tensor("wout", [E, E], F32, kind="ExternalInput").ap()
    d["bout"] = nc.dram_tensor("bout", [1, E], F32, kind="ExternalInput").ap()
    d["w8"] = nc.dram_tensor("w8", [E, E], I32, kind="ExternalInput").ap()
    d["ws"] = nc.dram_tensor("ws", [1, 1], F32, kind="ExternalInput").ap()
    d["fb"] = nc.dram_tensor("fb", [1, E], F32, kind="ExternalInput").ap()
    d["mask4"] = nc.dram_tensor("mask4", [128, 640], BF16, kind="ExternalInput").ap()
    d["ident"] = nc.dram_tensor("ident", [128, 128], BF16, kind="ExternalInput").ap()
    d_out = nc.dram_tensor("out", [S, E], F32, kind="ExternalOutput").ap()

    st = contextlib.ExitStack()
    sb = lambda n, s, dt_: st.enter_context(nc.sbuf_tensor(n, s, dt_))
    psa = lambda n, s, dt_: st.enter_context(nc.psum_tensor(n, s, dt_))
    with st:
        wqkv = sb("wqkvb", [128, NCH * 3 * E], BF16)
        wout = sb("woutb", [128, NCH * E], BF16)
        wfT = sb("wfT", [128, NCH * E], BF16)
        xbf = [sb("xbf0", [128, E], BF16)] * 2
        xfm = [sb(f"xfm{i}", [128, E], BF16) for i in range(2)]
        qfm = [sb(f"qfm{i}", [128, E], BF16) for i in range(2)]
        kfm = sb("kfm", [128, NCH * (KR + KMIR) * 128], BF16)
        vtm = sb("vtm", [128, KR * H * 65], BF16)
        n_q = sb("n_q", [128, E], BF16)
        n_k = sb("n_k", [128, E], BF16)
        sqb = sb("sqqk", [128, E], BF16)
        zzb = sb("zzb", [128, E], BF16)
        sq2 = sqb
        probs = [sb(f"probs{i}", [128, 640], BF16) for i in range(4)]
        probsT = [sb(f"probsT{i}", [128, 640], BF16) for i in range(4)]
        attn = [sb(f"attn{i}", [128, E], BF16) for i in range(2)]
        araw = sb("araw", [128, H * 65], BF16)
        afm = [sb("afm0", [128, E], BF16)] * 2
        q8tm = [sb(f"q8tm{i}", [128, E], BF16) for i in range(2)]
        q8fm = [sb("q8fm0", [128, E], BF16)] * 2
        xres = [sb(f"xres{i}", [128, E], F32) for i in range(2)]
        hb = [sb(f"hb{i}", [128, E], F32) for i in range(2)]
        x1b = [sb(f"x1b{i}", [128, E], F32) for i in range(2)]
        ry = sb("ry", [128, E], F32)
        rci = sb("rci", [128, E], I16)
        re0 = sb("re0", [128, E], BF16)
        re1 = sb("re1", [128, E], BF16)
        fbr = _Row(re0, 0, E)
        growb = _Row(re1, 0, 4 * D)
        grow = _Row(ry, 0, 4 * D)
        ones32 = _Row(ry, 512, 128)
        bqkvr = sb("bqkvr", [1, 3 * E], BF16)
        boutr = sb("boutr", [1, E], BF16)
        fbrep = sb("fbrep", [128, E], BF16)
        reps = {n: sb(n + "rep", [128, E], BF16)
                for n in ["gq", "bq", "gk", "bk"]}
        mask4 = sb("mask4s", [128, 640], BF16)
        ident = sb("idents", [128, 128], BF16)
        onesb = sb("onesb", [1, 128], BF16)
        wss = sb("wss", [1, 1], F32)
        cst = sb("cst", [128, 2], F32)
        fence = sb("fence", [128, 2], F32)
        wsr = sb("wsr", [128, 1], F32)
        ar = [sb(f"arena{i}", [128, 256], F32) for i in range(2)]
        lar = [sb(f"lar{i}", [128, 16], F32) for i in range(2)]
        negC = [sb(f"negC{i}", [128, 16], F32) for i in range(2)]
        rcp = [sb(f"rcp{i}", [128, 16], F32) for i in range(2)]

        big0 = psa("big0", [128, 1024], F32)
        big1 = psa("big1", [128, 1024], F32)
        scp = psa("scp", [128, 1024], F32)
        ptp = psa("ptp", [128, 1024], BF16)
        avp = psa("avp", [128, 512], F32)
        ptb = ptp.ap()

        S_ = Sched(nc, st)
        global _LAST_SCHED
        _LAST_SCHED = S_
        W, OP, INC = S_.W, S_.OP, S_.INC
        G = {"ptev": ("sDV", 0), "sqr": 0, "sqf": 0, "rysq": 0, "rsv": 0,
             "big0": 0, "big0k": "sDV", "big1": 0, "big1k": "sDV"}

        def FDV():
            return INC("vector", "sDV", lambda e: e.tensor_copy(
                out=fence.ap()[:, 0:1], in_=cst.ap()[:, 0:1]))

        def FAC():
            return INC("scalar", "sAC", lambda e: e.activation(
                out=fence.ap()[:, 1:2], in_=cst.ap()[:, 0:1], func=AF.Square))

        def transpose_group(src_ap_fn, nblk, wait_list, tag):
            for sem, val in wait_list:
                W("tensor", sem, val)
            W("tensor", G["ptev"][0], G["ptev"][1])
            vPE = None
            for c in range(nblk):
                src = src_ap_fn(c)
                fn = lambda e, src=src, c=c: e.transpose(
                    ptb[:, c * 128:(c + 1) * 128], src, ident.ap())
                if c == nblk - 1:
                    vPE = INC("tensor", "sPE", fn)
                else:
                    OP("tensor", fn)
            return vPE

        def pt_evac(dst_ap, w, vPE):
            W("vector", "sPE", vPE)
            v = INC("vector", "sDV", lambda e, a=dst_ap, w=w: e.tensor_copy(
                out=a, in_=ptb[:, 0:w]))
            G["ptev"] = ("sDV", v)
            return v

        vS_const = 0
        for dst, src in [(mask4.ap(), d["mask4"]), (ident.ap(), d["ident"]),
                         (grow.ap()[:, 0:D], d["gq"]),
                         (grow.ap()[:, D:2 * D], d["bq"]),
                         (grow.ap()[:, 2 * D:3 * D], d["gk"]),
                         (grow.ap()[:, 3 * D:4 * D], d["bk"]),
                         (wss.ap(), d["ws"])]:
            vS_const = INC("sync", "sC",
                           lambda e, a=dst, b=src: e.dma_start(out=a, in_=b), 16)
        vG_wqkv = INC("gpsimd", "sWQ", lambda e: e.dma_start(
            out=wqkv.ap().rearrange("p (k n) -> p k n", n=3 * E),
            in_=d["wqkv"].rearrange("(k p) n -> p k n", p=128)), 16)
        vG_wout = INC("gpsimd", "sWO", lambda e: e.dma_start(
            out=wout.ap().rearrange("p (k n) -> p k n", n=E),
            in_=d["wout"].rearrange("(k p) n -> p k n", p=128)), 16)
        vG_rows = 0
        for dst, src in [(bqkvr.ap(), d["bqkv"]), (boutr.ap(), d["bout"]),
                         (fbr.ap(), d["fb"])]:
            vG_rows = INC("gpsimd", "sRW",
                          lambda e, a=dst, b=src: e.dma_start(out=a, in_=b), 16)
        vG_x = {}
        vG_x[0] = INC("gpsimd", "sX", lambda e: e.dma_start(
            out=xbf[0].ap(), in_=d["x"][0:128, :]), 16)

        OP("vector", lambda e: e.memset(cst.ap()[:, 0:1], 1e-5))
        OP("vector", lambda e: e.memset(cst.ap()[:, 1:2], 1e-6))
        OP("vector", lambda e: e.memset(onesb.ap(), 1.0))
        OP("vector", lambda e: e.memset(ones32.ap(), 1.0))
        vDV_vones = INC("vector", "sDV", lambda e: e.memset(
            vtm.ap().rearrange("p (s h c) -> p s h c", h=H, c=65)
            [:, :, :, 64:65], 1.0))
        W("vector", "sC", vS_const)
        OP("vector", lambda e: e.tensor_single_scalar(
            out=grow.ap()[:, 0:2 * D], in_=grow.ap()[:, 0:2 * D],
            scalar=0.125, op=ALU.mult))
        vDV_growb = INC("vector", "sDV",
                        lambda e: e.tensor_copy(out=growb.ap(), in_=grow.ap()))

        W("tensor", "sDV", vDV_growb)
        vDV_reps = 0
        for idx, name in enumerate(["gq", "bq", "gk", "bk"]):
            rep = reps[name]
            vPE = INC("tensor", "sPE", lambda e, idx=idx: e.matmul(
                big0.ap()[:, 0:64], onesb.ap()[0:1, 0:128],
                growb.ap()[0:1, idx * D:(idx + 1) * D], start=True, stop=True))
            W("vector", "sPE", vPE)
            vDV = INC("vector", "sDV", lambda e, rep=rep: e.tensor_copy(
                out=rep.ap()[:, 0:64], in_=big0.ap()[:, 0:64]))
            W("tensor", "sDV", vDV)
            for s_ in range(1, 16):
                fn = lambda e, rep=rep, s_=s_: e.tensor_copy(
                    out=rep.ap()[:, s_ * 64:(s_ + 1) * 64],
                    in_=rep.ap()[:, 0:64])
                if s_ == 15 and name == "bk":
                    vDV_reps = INC("vector", "sDV", fn)
                else:
                    OP("vector", fn)

        vPE_ws = INC("tensor", "sPE", lambda e: e.matmul(
            big0.ap()[:, 64:65], ones32.ap()[0:1, 0:128], wss.ap(),
            start=True, stop=True))
        W("vector", "sPE", vPE_ws)
        vDV_wsr = INC("vector", "sDV", lambda e: e.tensor_copy(
            out=wsr.ap(), in_=big0.ap()[:, 64:65]))

        W("tensor", "sRW", vG_rows)
        W("tensor", "sDV", vDV_wsr)
        vPE_fb = 0
        for n in range(2):
            vPE_fb = INC("tensor", "sPE", lambda e, n=n: e.matmul(
                big0.ap()[:, n * 512:(n + 1) * 512], onesb.ap()[0:1, 0:128],
                fbr.ap()[0:1, n * 512:(n + 1) * 512], start=True, stop=True))
        W("vector", "sPE", vPE_fb)
        vDV_fbrep = INC("vector", "sDV", lambda e: e.tensor_copy(
            out=fbrep.ap(), in_=big0.ap()))
        G["big0"], G["big0k"] = vDV_fbrep, "sDV"

        vPE_wtr = 0
        vDV_cast = 0
        W("tensor", "sC", vS_const)
        for k in range(NCH):
            vDV_cast_h = 0
            for hf in range(2):
                W("sync", "sDV", vDV_cast if hf == 0 else vDV_cast_h)
                vS_w8 = INC("sync", "sW8", lambda e, k=k, hf=hf: e.dma_start(
                    out=rci.ap().bitcast(I32),
                    in_=d["w8"][k * 128:(k + 1) * 128,
                                hf * 512:(hf + 1) * 512]), 16)
                W("vector", "sW8", vS_w8)
                if hf == 0:
                    W("vector", "sPE", vPE_wtr)
                vDV_cast_h = INC("vector", "sDV", lambda e, hf=hf: e.tensor_copy(
                    out=q8tm[0].ap()[:, hf * 512:(hf + 1) * 512],
                    in_=rci.ap().bitcast(I32)))
            vDV_cast = vDV_cast_h
            vPE_wtr = transpose_group(
                lambda c: q8tm[0].ap()[:, c * 128:(c + 1) * 128], NCH,
                [("sDV", vDV_cast)], "w8")
            dst = wfT.ap().rearrange("p (c n) -> p c n", n=E)[
                :, :, k * 128:(k + 1) * 128]
            W("vector", "sPE", vPE_wtr)
            _v = INC("vector", "sDV", lambda e, dst=dst: e.tensor_copy(
                out=dst, in_=ptb.rearrange("p (c n) -> p c n", n=128)))
            G["ptev"] = ("sDV", _v)

        vDV_xfm, vDV_qfm, vDV_kfm, vDV_vev = {}, {}, {}, {}
        vPE_qkv_q, vPE_qkv_k, vPE_qkv_v = {}, {}, {}
        vAC_attn_done, vAC_x1, vDV_q8, vDV_fsc = {}, {}, {}, {}
        vS_xres, vPE_xtr_d, vAC_out_d, vPL_h1_d, sO_d = {}, {}, {}, {}, {}
        vDV_h0_d, vAC_y0_d = {}, {}
        vPE_qtr_d, vPE_ktr_d, vDV_nq, vDV_nk = {}, {}, {}, {}
        last_exp, last_avev, vPE_tr_u, vDV_pt_u, vPE_av_u = {}, {}, {}, {}, {}
        expPEgate = {}

        def x_transposes(tt):
            vPE = transpose_group(
                lambda c, tt=tt: xbf[tt % 2].ap()[:, c * 128:(c + 1) * 128],
                NCH, [("sX", vG_x[tt])], "x")
            vPE_xtr_d[tt] = vPE
            vDV_xfm[tt] = pt_evac(xfm[tt % 2].ap(), 1024, vPE)

        def qkv_wave(t, n0):
            xf = xfm[t % 2].ap()
            for kk in range(NCH):
                lhs = xf[:, kk * 128:(kk + 1) * 128]
                for j in range(2):
                    n = n0 + j
                    rhs = wqkv.ap()[:, kk * 3 * E + n * 512:
                                    kk * 3 * E + (n + 1) * 512]
                    ps = big1.ap()[:, j * 512:(j + 1) * 512]
                    OP("tensor", lambda e, l=lhs, r=rhs, ps=ps, kk=kk: e.matmul(
                        ps, l, r, start=(kk == 0), stop=False))
            v = None
            for j in range(2):
                n = n0 + j
                rb = bqkvr.ap()[0:1, n * 512:(n + 1) * 512]
                ps = big1.ap()[:, j * 512:(j + 1) * 512]
                v = INC("tensor", "sPE", lambda e, ps=ps, rb=rb: e.matmul(
                    ps, onesb.ap()[0:1, 0:128], rb, start=False, stop=True))
            return v

        def seg_stats(t, half, bigap, vPE_mm, eps):
            A = ar[t % 2].ap()
            o = 16 * half
            su, ss = A[:, 0 + o:16 + o], A[:, 32 + o:48 + o]
            mu, ms = A[:, 64 + o:80 + o], A[:, 96 + o:112 + o]
            vr, rs = A[:, 128 + o:144 + o], A[:, 160 + o:176 + o]
            W("vector", "sPE", vPE_mm)
            vDV_cp = INC("vector", "sDV", lambda e, b=bigap: e.tensor_copy(
                out=zzb.ap(), in_=b))
            INC("vector", "sDV", lambda e, su=su: e.tensor_reduce(
                out=su, in_=zzb.ap().rearrange("p (s dd) -> p s dd", dd=64),
                axis=AX.X, op=ALU.add))
            W("scalar", "sDV", vDV_cp)
            W("scalar", "sDV", G["sqr"])
            vAC_sq = INC("scalar", "sAC", lambda e: e.activation(
                out=sqb.ap(), in_=zzb.ap(), func=AF.Square))
            W("vector", "sAC", vAC_sq)
            G["sqr"] = INC("vector", "sDV", lambda e, ss=ss: e.tensor_reduce(
                out=ss, in_=sqb.ap().rearrange("p (s dd) -> p s dd", dd=64),
                axis=AX.X, op=ALU.add))
            OP("vector", lambda e, mu=mu, su=su: e.tensor_single_scalar(
                out=mu, in_=su, scalar=1.0 / 64, op=ALU.mult))
            OP("vector", lambda e, ms=ms, ss=ss: e.tensor_single_scalar(
                out=ms, in_=ss, scalar=1.0 / 64, op=ALU.mult))
            OP("vector", lambda e, vr=vr, mu=mu: e.tensor_mul(vr, mu, mu))
            INC("vector", "sDV",
                lambda e, vr=vr, ms=ms: e.tensor_sub(vr, ms, vr))
            vDV_var = FDV()
            W("scalar", "sDV", vDV_var)
            INC("scalar", "sAC", lambda e, vr=vr: e.activation(
                out=vr, in_=vr, func=AF.Ln, bias=cst.ap()[:, 0:1]))
            INC("scalar", "sAC", lambda e, rs=rs, vr=vr: e.activation(
                out=rs, in_=vr, func=AF.Exp, scale=-0.5))
            vAC_rs = FAC()
            return mu, rs, vAC_rs, vDV_cp

        def rstd_chain(L, c_in, c_out, epscol):
            W("scalar", "sDV", G["lnch"])
            INC("scalar", "sAC", lambda e, a=L[:, c_in:c_in + 1]:
                e.activation(out=a, in_=a, func=AF.Ln,
                             bias=cst.ap()[:, epscol:epscol + 1]))
            INC("scalar", "sAC",
                lambda e, o=L[:, c_out:c_out + 1],
                a=L[:, c_in:c_in + 1]: e.activation(out=o, in_=a,
                                                    func=AF.Exp, scale=-0.5))
            return FAC()

        G["lnch"] = 0

        def attn_geom(aa):
            nblk = min(aa + 1, 5)
            w = 128 * nblk
            kt0 = max(0, aa - 4)
            wrapped = aa >= KR and (aa % KR) < 4
            kcol0 = (kt0 % KR) * 128 if not wrapped else None
            return nblk, w, kt0, kcol0

        def stage_scores(aa, h):
            u = aa * H + h
            nblk, w, kt0, kcol0 = attn_geom(aa)
            scb = big0 if u % 2 == 0 else scp
            rows0 = 64 * (h % 2)
            ch = h // 2
            lhs = qfm[aa % 2].ap()[rows0:rows0 + 64, ch * 128:(ch + 1) * 128]
            kb = kfm.ap()[rows0:rows0 + 64,
                          ch * (KR + KMIR) * 128:(ch + 1) * (KR + KMIR) * 128]
            W("tensor", "sAC", last_exp.get(u - 2, 0))
            if kcol0 is not None:
                n1 = min(w, 512)
                r1 = kb[:, kcol0:kcol0 + n1]
                vPE_sc = INC("tensor", "sPE", lambda e, l=lhs, r=r1, n1=n1,
                             scb=scb: e.matmul(scb.ap()[:, 0:n1], l, r,
                                               start=True, stop=True))
                if w > 512:
                    r2 = kb[:, kcol0 + 512:kcol0 + w]
                    vPE_sc = INC("tensor", "sPE", lambda e, l=lhs, r=r2, w=w,
                                 scb=scb: e.matmul(scb.ap()[:, 512:w], l, r,
                                                   start=True, stop=True))
            else:
                vPE_sc = None
                for j in range(nblk):
                    kc = ((kt0 + j) % KR) * 128
                    rj = kb[:, kc:kc + 128]
                    fn = lambda e, l=lhs, r=rj, j=j, scb=scb: e.matmul(
                        scb.ap()[:, j * 128:(j + 1) * 128], l, r,
                        start=True, stop=True)
                    if j == nblk - 1:
                        vPE_sc = INC("tensor", "sPE", fn)
                    else:
                        OP("tensor", fn)
            cst = 256 if aa >= 4 else max(0, 128 * aa - 63)
            cw = 64 if aa >= 4 else min(64, 128 * aa + 1 - cst)
            W("vector", "sPE", vPE_sc)
            dstC = negC[aa % 2].ap()[:, h:h + 1]
            INC("vector", "sDV", lambda e, dstC=dstC, cst=cst, cw=cw, scb=scb:
                e.tensor_reduce(out=dstC, in_=scb.ap()[:, cst:cst + cw],
                                axis=AX.X, op=ALU.max, negate=True))
            vDV_mx = FDV()
            W("scalar", "sDV", vDV_mx)
            W("scalar", "sPE", expPEgate.get(u, 0))
            pb = probs[u % 4].ap()
            last_exp[u] = INC("scalar", "sAC", lambda e, pb=pb, w=w, dstC=dstC,
                              scb=scb: e.activation(
                out=pb[:, 0:w], in_=scb.ap()[:, 0:w],
                func=AF.Exp, bias=dstC, scale=1.0))
            moff = 0 if aa >= 4 else 512 - 128 * aa
            W("gpsimd", "sAC", last_exp[u])
            return INC("gpsimd", "sPL", lambda e, pb=pb, w=w, moff=moff:
                       e.tensor_mul(pb[:, 0:w], pb[:, 0:w],
                                    mask4.ap()[:, moff:moff + w]))

        def stage_transpose(aa, h, vPL_m):
            u = aa * H + h
            nblk, w, _, _ = attn_geom(aa)
            pb = probs[u % 4].ap()
            vPE = transpose_group(
                lambda c, pb=pb: pb[:, c * 128:(c + 1) * 128],
                nblk, [("sPL", vPL_m)], "pT")
            vPE_tr_u[u] = vPE
            dst = probsT[u % 4].ap()[:, 0:w]
            W("vector", "sPE", vPE_av_u.get(u - 4, 0))
            vDV_pt_u[u] = ("sDV", pt_evac(dst, w, vPE))
            expPEgate[u + 4] = vPE

        def stage_av(aa, h):
            u = aa * H + h
            nblk, w, kt0, _ = attn_geom(aa)
            W("tensor", "sAC", last_avev.get(u - 1, 0))
            W("tensor", vDV_pt_u[u][0], vDV_pt_u[u][1])
            pT = probsT[u % 4].ap()
            vPE_av = None
            for j in range(nblk):
                kt = kt0 + j
                vb = (kt % KR) * H * 65 + h * 65
                fn = lambda e, pT=pT, j=j, vb=vb, last=(j == nblk - 1): e.matmul(
                    avp.ap()[:, 0:65], pT[:, j * 128:(j + 1) * 128],
                    vtm.ap()[:, vb:vb + 65], start=(j == 0), stop=last)
                if j == nblk - 1:
                    vPE_av = INC("tensor", "sPE", fn)
                else:
                    OP("tensor", fn)
            vPE_av_u[u] = vPE_av
            W("scalar", "sPE", vPE_av)
            if h == 0:
                W("scalar", "sDV", G.get("anorm", 0))
            dstA = araw.ap()[:, h * 65:(h + 1) * 65]
            last_avev[u] = INC("scalar", "sAC", lambda e, dstA=dstA:
                               e.activation(out=dstA, in_=avp.ap()[:, 0:65],
                                            func=AF.Copy))

        def attn_half(aa, half):
            if half == 0:
                W("tensor", "sDV", vDV_qfm[aa])
                W("tensor", "sDV", vDV_kfm[aa])
                W("tensor", "sDV", vDV_vev[aa])
                G["attn_pend"] = []
            pend = G["attn_pend"]
            for h in range(8 * half, 8 * half + 8):
                m = stage_scores(aa, h)
                pend.append((h, m))
                if len(pend) >= 3:
                    hh, mm = pend.pop(0)
                    stage_transpose(aa, hh, mm)
                    if hh >= 2:
                        stage_av(aa, hh - 2)
                if h >= 2:
                    drain_chain(1)
            if half == 1:
                for hh, mm in pend:
                    stage_transpose(aa, hh, mm)
                for hh in range(H - 4, H):
                    stage_av(aa, hh)
                W("vector", "sAC", last_avev[aa * H + H - 1])
                a3 = araw.ap().rearrange("p (h c) -> p h c", c=65)
                INC("vector", "sDV", lambda e, a3=a3: e.reciprocal(
                    out=rcp[aa % 2].ap()[:, 0:16], in_=a3[:, :, 64:65]))
                rb_ = rcp[aa % 2].ap().rearrange(
                    "p (h u) -> p h u", u=1).broadcast_to([128, 16, 64])
                vno = INC("vector", "sDV", lambda e, a3=a3, rb_=rb_, aa=aa:
                          e.tensor_tensor(
                    out=attn[aa % 2].ap().rearrange("p (h c) -> p h c", c=64),
                    in0=a3[:, :, 0:64], in1=rb_, op=ALU.mult))
                G["anorm"] = vno
                vAC_attn_done[aa] = vno

        pending_chain = []

        def drain_chain(n=None):
            k = len(pending_chain) if n is None else min(n, len(pending_chain))
            for _ in range(k):
                pending_chain.pop(0)()

        def ln_stats_thunks(src_sb, L, epscol, pre_waits, x1b_gates):
            def t1():
                for eng, sem, val in pre_waits:
                    W(eng, sem, val())
                W("scalar", "sDV", G["sqf"])
                G["rysq"] = INC("scalar", "sAC", lambda e: e.activation(
                    out=ry.ap(), in_=src_sb.ap(), func=AF.Square))
                for fn in x1b_gates:
                    fn()
                INC("vector", "sDV", lambda e: e.tensor_reduce(
                    out=L[:, 0:1], in_=src_sb.ap(), axis=AX.X, op=ALU.add))

            def t2():
                W("vector", "sAC", G["rysq"])
                G["sqf"] = INC("vector", "sDV", lambda e: e.tensor_reduce(
                    out=L[:, 1:2], in_=ry.ap(), axis=AX.X, op=ALU.add))
                OP("vector", lambda e: e.tensor_single_scalar(
                    out=L[:, 2:3], in_=L[:, 0:1], scalar=1.0 / E, op=ALU.mult))
                OP("vector", lambda e: e.tensor_single_scalar(
                    out=L[:, 3:4], in_=L[:, 1:2], scalar=1.0 / E, op=ALU.mult))
                OP("vector", lambda e: e.tensor_mul(
                    L[:, 4:5], L[:, 2:3], L[:, 2:3]))
                INC("vector", "sDV", lambda e: e.tensor_sub(
                    L[:, 4:5], L[:, 3:4], L[:, 4:5]))
                G["lnch"] = FDV()
                G["rsv"] = rstd_chain(L, 4, 5, epscol)
            return [t1, t2]

        def make_pc_chain(p):
            L = lar[p % 2].ap()
            src = hb[p % 2]
            th = ln_stats_thunks(
                src, L, 0,
                pre_waits=[("scalar", "sDV", lambda p=p: vDV_h0_d[p])],
                x1b_gates=[
                    lambda p=p: W("vector", "sDV", vPL_h1_d.get(p - 2, 0)),
                    lambda p=p: W("vector", "sO", sO_d.get(p - 2, 0))])

            def t3():
                W("vector", "sAC", G["rsv"])
                vAC_x1[p] = INC("vector", "sDV", lambda e: e.tensor_scalar(
                    out=x1b[p % 2].ap(), in0=src.ap(), scalar1=L[:, 2:3],
                    scalar2=L[:, 5:6], op0=ALU.subtract, op1=ALU.mult))

            def t4():
                W("scalar", "sDV", vAC_x1[p])
                W("scalar", "sDV", G["sqf"])
                G["rysq"] = INC("scalar", "sAC", lambda e: e.activation(
                    out=ry.ap(), in_=x1b[p % 2].ap(), func=AF.Square))
                W("vector", "sAC", G["rysq"])
                G["sqf"] = INC("vector", "sDV", lambda e: e.tensor_reduce(
                    out=L[:, 12:13], in_=ry.ap(), axis=AX.X, op=ALU.add))
                OP("vector", lambda e: e.tensor_reduce(
                    out=L[:, 7:8], in_=x1b[p % 2].ap(), axis=AX.X, op=ALU.max,
                    apply_absolute_value=True))
                INC("vector", "sDV", lambda e: e.tensor_single_scalar(
                    out=L[:, 12:13], in_=L[:, 12:13], scalar=1.0 / E,
                    op=ALU.mult))
                G["lnch"] = FDV()
                G["rsv"] = rstd_chain(L, 12, 13, 1)

            def t5():
                W("vector", "sAC", G["rsv"])
                OP("vector", lambda e: e.tensor_mul(
                    L[:, 8:9], L[:, 7:8], L[:, 13:14]))
                OP("vector", lambda e: e.tensor_single_scalar(
                    out=L[:, 8:9], in_=L[:, 8:9], scalar=1e-5, op=ALU.max))
                OP("vector", lambda e: e.reciprocal(
                    out=L[:, 9:10], in_=L[:, 8:9]))
                OP("vector", lambda e: e.tensor_mul(
                    L[:, 10:11], L[:, 13:14], L[:, 9:10]))
                OP("vector", lambda e: e.tensor_single_scalar(
                    out=L[:, 10:11], in_=L[:, 10:11], scalar=127.0,
                    op=ALU.mult))
                W("vector", "sDV", vDV_wsr)
                OP("vector", lambda e: e.tensor_mul(
                    L[:, 11:12], L[:, 8:9], wsr.ap()))
                INC("vector", "sDV", lambda e: e.tensor_single_scalar(
                    out=L[:, 11:12], in_=L[:, 11:12], scalar=1.0 / 127,
                    op=ALU.mult))
                vDV_fsc[p] = FDV()

            def t6():
                OP("vector", lambda e: e.tensor_scalar(
                    out=ry.ap(), in0=x1b[p % 2].ap(), scalar1=L[:, 10:11],
                    scalar2=256.5, op0=ALU.mult, op1=ALU.add))
                OP("vector", lambda e: e.tensor_copy(
                    out=rci.ap(), in_=ry.ap()))
                G["sqf"] = INC("vector", "sDV", lambda e: e.tensor_tensor(
                    out=ry.ap(), in0=ry.ap(), in1=rci.ap(), op=ALU.subtract))

            def t7():
                OP("vector", lambda e: e.tensor_single_scalar(
                    out=re1.ap(), in_=ry.ap(), scalar=1.0, op=ALU.is_ge))
                OP("vector", lambda e: e.tensor_single_scalar(
                    out=re0.ap(), in_=ry.ap(), scalar=0.0, op=ALU.is_ge))
                OP("vector", lambda e: e.scalar_tensor_tensor(
                    out=ry.ap(), in0=rci.ap(), scalar=257.0, in1=re1.ap(),
                    op0=ALU.subtract, op1=ALU.add))
                G["sqf"] = vDV_q8[p] = INC(
                    "vector", "sDV", lambda e: e.tensor_tensor(
                        out=q8tm[p % 2].ap(), in0=ry.ap(), in1=re0.ap(),
                        op=ALU.add))
            return th + [t3, t4, t5, t6, t7]

        def make_fc_chain(f):
            L2 = lar[f % 2].ap()

            def ta():
                W("vector", "sAC", vAC_y0_d[f])
                OP("vector", lambda e: e.tensor_add(
                    hb[f % 2].ap(), hb[f % 2].ap(), fbrep.ap()))
                vPL_h1_d[f] = INC("vector", "sDV", lambda e: e.tensor_add(
                    hb[f % 2].ap(), hb[f % 2].ap(), x1b[f % 2].ap()))
            th = ln_stats_thunks(
                hb[f % 2], L2, 0,
                pre_waits=[("scalar", "sDV", lambda f=f: vPL_h1_d[f])],
                x1b_gates=[])

            def tb():
                W("vector", "sAC", G["rsv"])
                vAC_o = INC("vector", "sDV", lambda e: e.tensor_scalar(
                    out=x1b[f % 2].ap(), in0=hb[f % 2].ap(),
                    scalar1=L2[:, 2:3], scalar2=L2[:, 5:6],
                    op0=ALU.subtract, op1=ALU.mult))
                vAC_out_d[f] = vAC_o
                W("sync", "sDV", vAC_o)
                sO_d[f] = INC("sync", "sO", lambda e: e.dma_start(
                    out=d_out[f * 128:(f + 1) * 128, :],
                    in_=x1b[f % 2].ap()), 16)
            return [ta] + th + [tb]

        _dbg_rounds = int(os.environ.get("ATH_DEBUG_ROUNDS", NR))
        _skip = set(os.environ.get("ATH_SKIP", "").split(","))
        for r in range(_dbg_rounds):
            t, a, p, f = r, r - 2, r - 3, r - 4
            if "qkv" in _skip:
                t = NT
            if "attn" in _skip:
                a = -1
            if "proj" in _skip:
                p = -1
            if "ffn" in _skip:
                f = -1

            pc = r - 4 if "proj" not in _skip else -1
            fc = r - 5 if "ffn" not in _skip else -1
            if 0 <= pc < NT:
                pending_chain.extend(make_pc_chain(pc))
            if 0 <= fc < NT:
                pending_chain.extend(make_fc_chain(fc))

            if t == 0:
                x_transposes(0)
            if 1 <= t + 1 < NT:
                tt = t + 1
                W("gpsimd", "sPE", vPE_xtr_d[tt - 1])
                vG_x[tt] = INC("gpsimd", "sX", lambda e, tt=tt: e.dma_start(
                    out=xbf[0].ap(),
                    in_=d["x"][tt * 128:(tt + 1) * 128, :]), 16)
                x_transposes(tt)

            if t < NT:
                W("tensor", "sWQ", vG_wqkv)
                W("tensor", "sRW", vG_rows)
                W("tensor", "sDV", vDV_xfm[t])
                W("tensor", G["big1k"], G["big1"])
                W("tensor", "sDV", vDV_vones)
                vPE_qkv_v[t] = qkv_wave(t, 4)
                W("vector", "sPE", vPE_qkv_v[t])
                if t >= KR:
                    W("vector", "sDV", vAC_attn_done[t - (KR - 4)])
                slot = t % KR
                vv = None
                for h8 in range(2):
                    src_ = big1.ap()[:, h8 * 512:(h8 + 1) * 512].rearrange(
                        "p (h c) -> p h c", c=64)
                    base = slot * H * 65 + h8 * 8 * 65
                    dst = vtm.ap()[:, base:base + 8 * 65].rearrange(
                        "p (h c) -> p h c", c=65)[:, :, 0:64]
                    fn = lambda e, s=src_, dd=dst: e.tensor_copy(out=dd, in_=s)
                    if h8 == 1:
                        vv = INC("vector", "sDV", fn)
                    else:
                        OP("vector", fn)
                vDV_vev[t] = vv
                G["big1"], G["big1k"] = vv, "sDV"

            if t + 1 < NT and t >= 0:
                pass

            if t < NT:
                W("tensor", G["big1k"], G["big1"])
                vPE_qkv_q[t] = qkv_wave(t, 0)
                mu, rs, vAC_rsq, vDV_cpq = seg_stats(t, 0, big1.ap(),
                                                     vPE_qkv_q[t], 1e-5)
                G["big1"], G["big1k"] = vDV_cpq, "sDV"
            if 0 <= t - 1 < NT:
                tm1 = t - 1
                vPE_qtr = transpose_group(
                    lambda c, tm1=tm1: n_q.ap()[:, c * 128:(c + 1) * 128],
                    NCH, [("sDV", vDV_nq[tm1])], "q")
                vPE_qtr_d[tm1] = vPE_qtr
                vDV_qfm[tm1] = pt_evac(qfm[tm1 % 2].ap(), 1024, vPE_qtr)
            if 0 <= a < NT:
                attn_half(a, 0)

            if t < NT:
                W("vector", "sAC", vAC_rsq)
                W("vector", "sPE", vPE_qtr_d.get(t - 1, 0))
                zz3 = zzb.ap().rearrange("p (s u) -> p s u", u=64)
                nq3 = n_q.ap().rearrange("p (s u) -> p s u", u=64)
                mu_b = mu.rearrange("p (s u) -> p s u", u=1).broadcast_to(
                    [128, 16, 64])
                rs_b = rs.rearrange("p (s u) -> p s u", u=1).broadcast_to(
                    [128, 16, 64])
                OP("vector", lambda e, mu_b=mu_b, zz3=zz3, nq3=nq3:
                   e.tensor_tensor(out=nq3, in0=zz3, in1=mu_b,
                                   op=ALU.subtract))
                OP("vector", lambda e, rs_b=rs_b, nq3=nq3:
                   e.tensor_tensor(out=nq3, in0=nq3, in1=rs_b, op=ALU.mult))
                W("vector", "sDV", vDV_reps)
                OP("vector", lambda e: e.tensor_mul(
                    n_q.ap(), n_q.ap(), reps["gq"].ap()))
                vDV_nq[t] = INC("vector", "sDV", lambda e: e.tensor_add(
                    n_q.ap(), n_q.ap(), reps["bq"].ap()))

            if t < NT:
                W("tensor", G["big1k"], G["big1"])
                vPE_qkv_k[t] = qkv_wave(t, 2)
                mu2, rs2, vAC_rsk, vDV_cpk = seg_stats(t, 1, big1.ap(),
                                                       vPE_qkv_k[t], 1e-5)
                G["big1"], G["big1k"] = vDV_cpk, "sDV"
            if 0 <= t - 1 < NT:
                tm1 = t - 1
                vPE_ktr = transpose_group(
                    lambda c: n_k.ap()[:, c * 128:(c + 1) * 128], NCH,
                    [("sDV", vDV_nk[tm1])], "k")
                vPE_ktr_d[tm1] = vPE_ktr
                slot = tm1 % KR
                W("vector", "sPE", vPE_ktr)
                dst = kfm.ap().rearrange(
                    "p (c w_) -> p c w_", w_=(KR + KMIR) * 128)[
                    :, :, slot * 128:(slot + 1) * 128]
                vlast = INC("vector", "sDV", lambda e, dst=dst: e.tensor_copy(
                    out=dst, in_=ptb.rearrange("p (c w_) -> p c w_", w_=128)))
                vDV_kfm[tm1] = vlast
                G["ptev"] = ("sDV", vlast)
            if 0 <= a < NT:
                attn_half(a, 1)

            if t < NT:
                W("vector", "sAC", vAC_rsk)
                W("vector", "sPE", vPE_ktr_d.get(t - 1, 0))
                zz3 = zzb.ap().rearrange("p (s u) -> p s u", u=64)
                nk3 = n_k.ap().rearrange("p (s u) -> p s u", u=64)
                mu2_b = mu2.rearrange("p (s u) -> p s u", u=1).broadcast_to(
                    [128, 16, 64])
                rs2_b = rs2.rearrange("p (s u) -> p s u", u=1).broadcast_to(
                    [128, 16, 64])
                OP("vector", lambda e, mu2_b=mu2_b, zz3=zz3, nk3=nk3:
                   e.tensor_tensor(out=nk3, in0=zz3, in1=mu2_b,
                                   op=ALU.subtract))
                OP("vector", lambda e, rs2_b=rs2_b, nk3=nk3:
                   e.tensor_tensor(out=nk3, in0=nk3, in1=rs2_b, op=ALU.mult))
                OP("vector", lambda e: e.tensor_mul(
                    n_k.ap(), n_k.ap(), reps["gk"].ap()))
                vDV_nk[t] = INC("vector", "sDV", lambda e: e.tensor_add(
                    n_k.ap(), n_k.ap(), reps["bk"].ap()))

            drain_chain()

            if 0 <= p < NT:
                vPE_atr = transpose_group(
                    lambda c, p=p: attn[p % 2].ap()[:, c * 128:(c + 1) * 128],
                    NCH, [("sDV", vAC_attn_done[p])], "a")
                vDV_afm = pt_evac(afm[p % 2].ap(), 1024, vPE_atr)
                W("tensor", G["big1k"], G["big1"])
                W("tensor", "sWO", vG_wout)
                W("tensor", "sDV", vDV_afm)
                af = afm[p % 2].ap()
                vPE_proj = None
                for n in range(2):
                    for kk in range(NCH):
                        l = af[:, kk * 128:(kk + 1) * 128]
                        rr = wout.ap()[:, kk * E + n * 512:kk * E + (n + 1) * 512]
                        OP("tensor", lambda e, l=l, rr=rr, n=n, kk=kk: e.matmul(
                            big1.ap()[:, n * 512:(n + 1) * 512], l, rr,
                            start=(kk == 0), stop=False))
                    rb = boutr.ap()[0:1, n * 512:(n + 1) * 512]
                    vPE_proj = INC("tensor", "sPE", lambda e, rb=rb, n=n: e.matmul(
                        big1.ap()[:, n * 512:(n + 1) * 512],
                        onesb.ap()[0:1, 0:128], rb, start=False, stop=True))
                W("vector", "sPE", vPE_proj)
                W("vector", "sXR%d" % (p % 2), vS_xres[p])
                W("vector", "sDV", vAC_out_d.get(p - 2, 0))
                vDV_h0 = INC("vector", "sDV", lambda e, p=p: e.tensor_add(
                    hb[p % 2].ap(), big1.ap(), xres[p % 2].ap()))
                vDV_h0_d[p] = vDV_h0
                G["big1"], G["big1k"] = vDV_h0, "sDV"

            drain_chain()

            if 0 <= f < NT:
                vPE_q8t = transpose_group(
                    lambda c, f=f: q8tm[f % 2].ap()[:, c * 128:(c + 1) * 128],
                    NCH, [("sDV", vDV_q8[f])], "q8")
                vDV_q8f = pt_evac(q8fm[f % 2].ap(), 1024, vPE_q8t)
                W("tensor", G["big1k"], G["big1"])
                W("tensor", "sDV", vDV_q8f)
                qf = q8fm[f % 2].ap()
                vPE_ffn = None
                for n in range(2):
                    for kk in range(NCH):
                        l = qf[:, kk * 128:(kk + 1) * 128]
                        rr = wfT.ap()[:, kk * E + n * 512:kk * E + (n + 1) * 512]
                        fn = lambda e, l=l, rr=rr, n=n, kk=kk: e.matmul(
                            big1.ap()[:, n * 512:(n + 1) * 512], l, rr,
                            start=(kk == 0), stop=(kk == NCH - 1))
                        if n == 1 and kk == NCH - 1:
                            vPE_ffn = INC("tensor", "sPE", fn)
                        else:
                            OP("tensor", fn)
                L2 = lar[f % 2].ap()
                W("scalar", "sPE", vPE_ffn)
                W("scalar", "sDV", vDV_fsc[f])
                W("scalar", "sDV", vAC_x1[f])
                vAC_y0 = INC("scalar", "sAC", lambda e, f=f, L2=L2:
                             e.activation(out=hb[f % 2].ap(), in_=big1.ap(),
                                          func=AF.Copy,
                                          scale=L2[:, 11:12]))
                vAC_y0_d[f] = vAC_y0
                G["big1"], G["big1k"] = vAC_y0, "sAC"

            tp = p + 1
            if 0 <= tp < NT:
                if tp >= 2:
                    W("sync", "sDV", vAC_x1[tp - 2])
                vS_xres[tp] = INC("sync", "sXR%d" % (tp % 2),
                                  lambda e, tp=tp: e.dma_start(
                    out=xres[tp % 2].ap(),
                    in_=d["x"][tp * 128:(tp + 1) * 128, :]), 16)

        if os.environ.get("ATH_DUMP"):
            dumps = [("d_nq", n_q), ("d_nk", n_k), ("d_xfm0", xfm[0]),
                     ("d_qfm0", qfm[0]), ("d_kfm", kfm), ("d_vtm", vtm),
                     ("d_attn0", attn[0]), ("d_attn1", attn[1]),
                     ("d_probs0", probs[0]), ("d_probsT0", probsT[0]),
                     ("d_x1b0", x1b[0]), ("d_hb0", hb[0]),
                     ("d_q8tm0", q8tm[0]), ("d_wfT", wfT),
                     ("d_negC0", negC[0]), ("d_rcp0", rcp[0]),
                     ("d_ar0", ar[0]), ("d_lar0", lar[0]),
                     ("d_fbrep", fbrep), ("d_gqrep", reps["gq"]),
                     ("d_wqkv", wqkv)]
            for sname in ["sPE", "sDV", "sAC", "sPL"]:
                W("sync", sname, S_.cnt[sname])
            for dn, tens in dumps:
                shape = list(tens.ap().shape)
                dt_ = tens.ap().dtype
                dram = nc.dram_tensor(dn, shape, dt_, kind="ExternalOutput").ap()
                INC("sync", "sO", lambda e, a=dram, b=tens: e.dma_start(
                    out=a, in_=b.ap()), 16)
        W("sync", "sO", S_.cnt["sO"])

        with nc.Block() as block:
            @block.sync
            def _(e):
                for fn in S_.plan["sync"]:
                    fn(e)

            @block.tensor
            def _(e):
                for fn in S_.plan["tensor"]:
                    fn(e)

            @block.vector
            def _(e):
                for fn in S_.plan["vector"]:
                    fn(e)

            @block.scalar
            def _(e):
                for fn in S_.plan["scalar"]:
                    fn(e)

            @block.gpsimd
            def _(e):
                for fn in S_.plan["gpsimd"]:
                    fn(e)

    return nc


def kernel(L0_attn_x, L0_attn_W_qkv, L0_attn_b_qkv, L0_attn_q_gamma,
           L0_attn_q_beta, L0_attn_k_gamma, L0_attn_k_beta, L0_attn_W_out,
           L0_attn_b_out, L0_ffn_w_int8, L0_ffn_weight_scale, L0_ffn_bias):
    nc = build_nc()
    x = np.ascontiguousarray(np.asarray(L0_attn_x, dtype=np.float32))
    com = {
        "wqkv": np.ascontiguousarray(np.asarray(L0_attn_W_qkv, np.float32)),
        "bqkv": np.asarray(L0_attn_b_qkv, np.float32).reshape(1, -1),
        "gq": np.asarray(L0_attn_q_gamma, np.float32).reshape(1, -1),
        "bq": np.asarray(L0_attn_q_beta, np.float32).reshape(1, -1),
        "gk": np.asarray(L0_attn_k_gamma, np.float32).reshape(1, -1),
        "bk": np.asarray(L0_attn_k_beta, np.float32).reshape(1, -1),
        "wout": np.ascontiguousarray(np.asarray(L0_attn_W_out, np.float32)),
        "bout": np.asarray(L0_attn_b_out, np.float32).reshape(1, -1),
        "w8": np.ascontiguousarray(np.asarray(L0_ffn_w_int8, np.int32)),
        "ws": np.asarray(L0_ffn_weight_scale, np.float32).reshape(1, 1),
        "fb": np.asarray(L0_ffn_bias, np.float32).reshape(1, -1),
        "mask4": _mask4_np(),
        "ident": np.eye(128, dtype=np.float32).astype(BF16NP),
    }
    in_maps = [dict(com, x=np.ascontiguousarray(x[b])) for b in range(8)]
    res = run_bass_kernel_spmd(nc, in_maps, core_ids=list(range(8)))
    return np.stack([np.asarray(r["out"], dtype=np.float32)
                     for r in res.results], axis=0)


# version 20 (final)
# speedup vs baseline: 1.0643x; 1.0400x over previous
import os
import numpy as np
import ml_dtypes
import contextlib
import concourse.bass as bass
import concourse.mybir as mybir
from concourse.bass_utils import run_bass_kernel_spmd

F32 = mybir.dt.float32
BF16 = mybir.dt.bfloat16
I32 = mybir.dt.int32
I16 = mybir.dt.int16
AF = mybir.ActivationFunctionType
ALU = mybir.AluOpType
AX = mybir.AxisListType

S = 2048
E = 1024
H = 16
D = 64
NT = S // 128
NCH = E // 128
KR = 7
KMIR = 0
NR = NT + 5
BF16NP = ml_dtypes.bfloat16
_LAST_SCHED = None


class Sched:
    def __init__(self, nc, stack):
        self.nc = nc
        names = ["sS", "sG", "sPE", "sDV", "sAC", "sPL", "sO",
                 "sC", "sW8", "sXR0", "sXR1", "sWQ", "sWO", "sRW", "sX"]
        self.sem = {n: stack.enter_context(nc.semaphore(n)) for n in names}
        self.cnt = {n: 0 for n in names}
        self.plan = {n: [] for n in
                     ["sync", "tensor", "vector", "scalar", "gpsimd"]}
        self.trace = {n: [] for n in
                      ["sync", "tensor", "vector", "scalar", "gpsimd"]}

    AUTO_EDGE = {"vector": "sDV", "scalar": "sAC", "gpsimd": "sPL"}

    def W(self, eng, sem, val):
        if val and val > 0:
            h = self.sem[sem]
            self.trace[eng].append(("w", sem, val))
            self.plan[eng].append(lambda e, h=h, v=val: e.wait_ge(h, v))

    def _self_edge(self, eng):
        sem = self.AUTO_EDGE.get(eng)
        if sem is not None and self.cnt[sem] > 0:
            self.W(eng, sem, self.cnt[sem])

    def OP(self, eng, fn):
        sem = self.AUTO_EDGE.get(eng)
        if sem is not None:
            self.INC(eng, sem, fn)
        else:
            self.trace[eng].append(("op",))
            self.plan[eng].append(fn)

    def INC(self, eng, sem, fn, k=1):
        self._self_edge(eng)
        self.cnt[sem] += k
        h = self.sem[sem]
        self.trace[eng].append(("i", sem, k))
        self.plan[eng].append(
            lambda e, fn=fn, h=h, k=k: fn(e).then_inc(h, k))
        return self.cnt[sem]


class _Row:

    def __init__(self, base, off, n):
        self.base, self.off, self.n = base, off, n

    def ap(self):
        return self.base.ap()[0:1, self.off:self.off + self.n]


def _mask4_np():
    i = np.arange(128)[:, None]
    c = np.arange(640)[None, :]
    return ((c >= i + 1) & (c <= i + 512)).astype(np.float32).astype(BF16NP)


def build_nc():
    nc = bass.Bass("TRN2", target_bir_lowering=False)
    d = {}
    d["x"] = nc.dram_tensor("x", [S, E], F32, kind="ExternalInput").ap()
    d["wqkv"] = nc.dram_tensor("wqkv", [E, 3 * E], F32, kind="ExternalInput").ap()
    d["bqkv"] = nc.dram_tensor("bqkv", [1, 3 * E], F32, kind="ExternalInput").ap()
    for n in ["gq", "bq", "gk", "bk"]:
        d[n] = nc.dram_tensor(n, [1, D], F32, kind="ExternalInput").ap()
    d["wout"] = nc.dram_tensor("wout", [E, E], F32, kind="ExternalInput").ap()
    d["bout"] = nc.dram_tensor("bout", [1, E], F32, kind="ExternalInput").ap()
    d["w8"] = nc.dram_tensor("w8", [E, E], I32, kind="ExternalInput").ap()
    d["ws"] = nc.dram_tensor("ws", [1, 1], F32, kind="ExternalInput").ap()
    d["fb"] = nc.dram_tensor("fb", [1, E], F32, kind="ExternalInput").ap()
    d["mask4"] = nc.dram_tensor("mask4", [128, 640], BF16, kind="ExternalInput").ap()
    d["ident"] = nc.dram_tensor("ident", [128, 128], BF16, kind="ExternalInput").ap()
    d_out = nc.dram_tensor("out", [S, E], F32, kind="ExternalOutput").ap()

    st = contextlib.ExitStack()
    sb = lambda n, s, dt_: st.enter_context(nc.sbuf_tensor(n, s, dt_))
    psa = lambda n, s, dt_: st.enter_context(nc.psum_tensor(n, s, dt_))
    with st:
        wqkv = sb("wqkvb", [128, NCH * 3 * E], BF16)
        wout = sb("woutb", [128, NCH * E], BF16)
        wfT = sb("wfT", [128, NCH * E], BF16)
        xbf = [sb("xbf0", [128, E], BF16)] * 2
        xfm = [sb(f"xfm{i}", [128, E], BF16) for i in range(2)]
        qfm = [sb(f"qfm{i}", [128, E], BF16) for i in range(2)]
        kfm = sb("kfm", [128, NCH * (KR + KMIR) * 128], BF16)
        vtm = sb("vtm", [128, KR * H * 65], BF16)
        n_q = sb("n_q", [128, E], BF16)
        n_k = sb("n_k", [128, E], BF16)
        sqb = sb("sqqk", [128, E], BF16)
        zzb = sb("zzb", [128, E], BF16)
        sq2 = sqb
        probs = [sb(f"probs{i}", [128, 640], BF16) for i in range(4)]
        probsT = [sb(f"probsT{i}", [128, 640], BF16) for i in range(4)]
        attn = [sb(f"attn{i}", [128, E], BF16) for i in range(2)]
        araw = sb("araw", [128, H * 65], BF16)
        afm = [sb("afm0", [128, E], BF16)] * 2
        q8tm = [sb(f"q8tm{i}", [128, E], BF16) for i in range(2)]
        q8fm = [sb("q8fm0", [128, E], BF16)] * 2
        xres = [sb(f"xres{i}", [128, E], F32) for i in range(2)]
        hb = [sb(f"hb{i}", [128, E], F32) for i in range(2)]
        x1b = [sb(f"x1b{i}", [128, E], F32) for i in range(2)]
        ry = sb("ry", [128, E], F32)
        rci = sb("rci", [128, E], I16)
        re0 = sb("re0", [128, E], BF16)
        re1 = sb("re1", [128, E], BF16)
        fbr = _Row(re0, 0, E)
        growb = _Row(re1, 0, 4 * D)
        grow = _Row(ry, 0, 4 * D)
        ones32 = _Row(ry, 512, 128)
        bqkvr = sb("bqkvr", [1, 3 * E], BF16)
        boutr = sb("boutr", [1, E], BF16)
        fbrep = sb("fbrep", [128, E], BF16)
        reps = {n: sb(n + "rep", [128, E], BF16)
                for n in ["gq", "bq", "gk", "bk"]}
        mask4 = sb("mask4s", [128, 640], BF16)
        ident = sb("idents", [128, 128], BF16)
        onesb = sb("onesb", [1, 128], BF16)
        wss = sb("wss", [1, 1], F32)
        cst = sb("cst", [128, 2], F32)
        fence = sb("fence", [128, 2], F32)
        wsr = sb("wsr", [128, 1], F32)
        ar = [sb(f"arena{i}", [128, 256], F32) for i in range(2)]
        lar = [sb(f"lar{i}", [128, 16], F32) for i in range(2)]
        negC = [sb(f"negC{i}", [128, 16], F32) for i in range(2)]
        rcp = [sb(f"rcp{i}", [128, 16], F32) for i in range(2)]

        big0 = psa("big0", [128, 1024], F32)
        big1 = psa("big1", [128, 1024], F32)
        scp = psa("scp", [128, 1024], F32)
        ptp = psa("ptp", [128, 1024], BF16)
        avp = psa("avp", [128, 512], F32)
        ptb = ptp.ap()

        S_ = Sched(nc, st)
        global _LAST_SCHED
        _LAST_SCHED = S_
        W, OP, INC = S_.W, S_.OP, S_.INC
        G = {"ptev": ("sDV", 0), "sqr": 0, "sqf": 0, "rysq": 0, "rsv": 0,
             "big0": 0, "big0k": "sDV", "big1": 0, "big1k": "sDV"}

        def FDV():
            return INC("vector", "sDV", lambda e: e.tensor_copy(
                out=fence.ap()[:, 0:1], in_=cst.ap()[:, 0:1]))

        def FAC():
            return INC("scalar", "sAC", lambda e: e.activation(
                out=fence.ap()[:, 1:2], in_=cst.ap()[:, 0:1], func=AF.Square))

        def transpose_group(src_ap_fn, nblk, wait_list, tag):
            for sem, val in wait_list:
                W("tensor", sem, val)
            W("tensor", G["ptev"][0], G["ptev"][1])
            vPE = None
            for c in range(nblk):
                src = src_ap_fn(c)
                fn = lambda e, src=src, c=c: e.transpose(
                    ptb[:, c * 128:(c + 1) * 128], src, ident.ap())
                if c == nblk - 1:
                    vPE = INC("tensor", "sPE", fn)
                else:
                    OP("tensor", fn)
            return vPE

        def pt_evac(dst_ap, w, vPE):
            W("vector", "sPE", vPE)
            v = INC("vector", "sDV", lambda e, a=dst_ap, w=w: e.tensor_copy(
                out=a, in_=ptb[:, 0:w]))
            G["ptev"] = ("sDV", v)
            return v

        vS_const = 0
        for dst, src in [(mask4.ap(), d["mask4"]), (ident.ap(), d["ident"]),
                         (grow.ap()[:, 0:D], d["gq"]),
                         (grow.ap()[:, D:2 * D], d["bq"]),
                         (grow.ap()[:, 2 * D:3 * D], d["gk"]),
                         (grow.ap()[:, 3 * D:4 * D], d["bk"]),
                         (wss.ap(), d["ws"])]:
            vS_const = INC("sync", "sC",
                           lambda e, a=dst, b=src: e.dma_start(out=a, in_=b), 16)
        vG_wqkv = INC("gpsimd", "sWQ", lambda e: e.dma_start(
            out=wqkv.ap().rearrange("p (k n) -> p k n", n=3 * E),
            in_=d["wqkv"].rearrange("(k p) n -> p k n", p=128)), 16)
        vG_wout = INC("gpsimd", "sWO", lambda e: e.dma_start(
            out=wout.ap().rearrange("p (k n) -> p k n", n=E),
            in_=d["wout"].rearrange("(k p) n -> p k n", p=128)), 16)
        vG_rows = 0
        for dst, src in [(bqkvr.ap(), d["bqkv"]), (boutr.ap(), d["bout"]),
                         (fbr.ap(), d["fb"])]:
            vG_rows = INC("gpsimd", "sRW",
                          lambda e, a=dst, b=src: e.dma_start(out=a, in_=b), 16)
        vG_x = {}
        vG_x[0] = INC("gpsimd", "sX", lambda e: e.dma_start(
            out=xbf[0].ap(), in_=d["x"][0:128, :]), 16)

        OP("vector", lambda e: e.memset(cst.ap()[:, 0:1], 1e-5))
        OP("vector", lambda e: e.memset(cst.ap()[:, 1:2], 1e-6))
        OP("vector", lambda e: e.memset(onesb.ap(), 1.0))
        OP("vector", lambda e: e.memset(ones32.ap(), 1.0))
        vDV_vones = INC("vector", "sDV", lambda e: e.memset(
            vtm.ap().rearrange("p (s h c) -> p s h c", h=H, c=65)
            [:, :, :, 64:65], 1.0))
        W("vector", "sC", vS_const)
        OP("vector", lambda e: e.tensor_single_scalar(
            out=grow.ap()[:, 0:2 * D], in_=grow.ap()[:, 0:2 * D],
            scalar=0.125, op=ALU.mult))
        vDV_growb = INC("vector", "sDV",
                        lambda e: e.tensor_copy(out=growb.ap(), in_=grow.ap()))

        W("tensor", "sDV", vDV_growb)
        vDV_reps = 0
        for idx, name in enumerate(["gq", "bq", "gk", "bk"]):
            rep = reps[name]
            vPE = INC("tensor", "sPE", lambda e, idx=idx: e.matmul(
                big0.ap()[:, 0:64], onesb.ap()[0:1, 0:128],
                growb.ap()[0:1, idx * D:(idx + 1) * D], start=True, stop=True))
            W("vector", "sPE", vPE)
            vDV = INC("vector", "sDV", lambda e, rep=rep: e.tensor_copy(
                out=rep.ap()[:, 0:64], in_=big0.ap()[:, 0:64]))
            W("tensor", "sDV", vDV)
            for s_ in range(1, 16):
                fn = lambda e, rep=rep, s_=s_: e.tensor_copy(
                    out=rep.ap()[:, s_ * 64:(s_ + 1) * 64],
                    in_=rep.ap()[:, 0:64])
                if s_ == 15 and name == "bk":
                    vDV_reps = INC("vector", "sDV", fn)
                else:
                    OP("vector", fn)

        vPE_ws = INC("tensor", "sPE", lambda e: e.matmul(
            big0.ap()[:, 64:65], ones32.ap()[0:1, 0:128], wss.ap(),
            start=True, stop=True))
        W("vector", "sPE", vPE_ws)
        vDV_wsr = INC("vector", "sDV", lambda e: e.tensor_copy(
            out=wsr.ap(), in_=big0.ap()[:, 64:65]))

        W("tensor", "sRW", vG_rows)
        W("tensor", "sDV", vDV_wsr)
        vPE_fb = 0
        for n in range(2):
            vPE_fb = INC("tensor", "sPE", lambda e, n=n: e.matmul(
                big0.ap()[:, n * 512:(n + 1) * 512], onesb.ap()[0:1, 0:128],
                fbr.ap()[0:1, n * 512:(n + 1) * 512], start=True, stop=True))
        W("vector", "sPE", vPE_fb)
        vDV_fbrep = INC("vector", "sDV", lambda e: e.tensor_copy(
            out=fbrep.ap(), in_=big0.ap()))
        G["big0"], G["big0k"] = vDV_fbrep, "sDV"

        vPE_wtr = 0
        vDV_cast = 0
        W("tensor", "sC", vS_const)
        for k in range(NCH):
            vDV_cast_h = 0
            for hf in range(2):
                W("sync", "sDV", vDV_cast if hf == 0 else vDV_cast_h)
                vS_w8 = INC("sync", "sW8", lambda e, k=k, hf=hf: e.dma_start(
                    out=rci.ap().bitcast(I32),
                    in_=d["w8"][k * 128:(k + 1) * 128,
                                hf * 512:(hf + 1) * 512]), 16)
                W("vector", "sW8", vS_w8)
                if hf == 0:
                    W("vector", "sPE", vPE_wtr)
                vDV_cast_h = INC("vector", "sDV", lambda e, hf=hf: e.tensor_copy(
                    out=q8tm[0].ap()[:, hf * 512:(hf + 1) * 512],
                    in_=rci.ap().bitcast(I32)))
            vDV_cast = vDV_cast_h
            vPE_wtr = transpose_group(
                lambda c: q8tm[0].ap()[:, c * 128:(c + 1) * 128], NCH,
                [("sDV", vDV_cast)], "w8")
            dst = wfT.ap().rearrange("p (c n) -> p c n", n=E)[
                :, :, k * 128:(k + 1) * 128]
            W("vector", "sPE", vPE_wtr)
            _v = INC("vector", "sDV", lambda e, dst=dst: e.tensor_copy(
                out=dst, in_=ptb.rearrange("p (c n) -> p c n", n=128)))
            G["ptev"] = ("sDV", _v)

        vDV_xfm, vDV_qfm, vDV_kfm, vDV_vev = {}, {}, {}, {}
        vPE_qkv_q, vPE_qkv_k, vPE_qkv_v = {}, {}, {}
        vAC_attn_done, vAC_x1, vDV_q8, vDV_fsc = {}, {}, {}, {}
        vS_xres, vPE_xtr_d, vAC_out_d, vPL_h1_d, sO_d = {}, {}, {}, {}, {}
        vDV_h0_d, vAC_y0_d = {}, {}
        vPE_qtr_d, vPE_ktr_d, vDV_nq, vDV_nk = {}, {}, {}, {}
        last_exp, last_avev, vPE_tr_u, vDV_pt_u, vPE_av_u = {}, {}, {}, {}, {}
        expPEgate = {}

        def x_transposes(tt):
            vPE = transpose_group(
                lambda c, tt=tt: xbf[tt % 2].ap()[:, c * 128:(c + 1) * 128],
                NCH, [("sX", vG_x[tt])], "x")
            vPE_xtr_d[tt] = vPE
            vDV_xfm[tt] = pt_evac(xfm[tt % 2].ap(), 1024, vPE)

        def qkv_wave(t, n0):
            xf = xfm[t % 2].ap()
            for kk in range(NCH):
                lhs = xf[:, kk * 128:(kk + 1) * 128]
                for j in range(2):
                    n = n0 + j
                    rhs = wqkv.ap()[:, kk * 3 * E + n * 512:
                                    kk * 3 * E + (n + 1) * 512]
                    ps = big1.ap()[:, j * 512:(j + 1) * 512]
                    OP("tensor", lambda e, l=lhs, r=rhs, ps=ps, kk=kk: e.matmul(
                        ps, l, r, start=(kk == 0), stop=False))
            v = None
            for j in range(2):
                n = n0 + j
                rb = bqkvr.ap()[0:1, n * 512:(n + 1) * 512]
                ps = big1.ap()[:, j * 512:(j + 1) * 512]
                v = INC("tensor", "sPE", lambda e, ps=ps, rb=rb: e.matmul(
                    ps, onesb.ap()[0:1, 0:128], rb, start=False, stop=True))
            return v

        def seg_stats(t, half, bigap, vPE_mm, eps):
            A = ar[t % 2].ap()
            o = 16 * half
            su, ss = A[:, 0 + o:16 + o], A[:, 32 + o:48 + o]
            mu, ms = A[:, 64 + o:80 + o], A[:, 96 + o:112 + o]
            vr, rs = A[:, 128 + o:144 + o], A[:, 160 + o:176 + o]
            W("vector", "sPE", vPE_mm)
            vDV_cp = INC("vector", "sDV", lambda e, b=bigap: e.tensor_copy(
                out=zzb.ap(), in_=b))
            INC("vector", "sDV", lambda e, su=su: e.tensor_reduce(
                out=su, in_=zzb.ap().rearrange("p (s dd) -> p s dd", dd=64),
                axis=AX.X, op=ALU.add))
            W("scalar", "sDV", vDV_cp)
            W("scalar", "sDV", G["sqr"])
            vAC_sq = INC("scalar", "sAC", lambda e: e.activation(
                out=sqb.ap(), in_=zzb.ap(), func=AF.Square))
            W("vector", "sAC", vAC_sq)
            G["sqr"] = INC("vector", "sDV", lambda e, ss=ss: e.tensor_reduce(
                out=ss, in_=sqb.ap().rearrange("p (s dd) -> p s dd", dd=64),
                axis=AX.X, op=ALU.add))
            OP("vector", lambda e, mu=mu, su=su: e.tensor_single_scalar(
                out=mu, in_=su, scalar=1.0 / 64, op=ALU.mult))
            OP("vector", lambda e, ms=ms, ss=ss: e.tensor_single_scalar(
                out=ms, in_=ss, scalar=1.0 / 64, op=ALU.mult))
            OP("vector", lambda e, vr=vr, mu=mu: e.tensor_mul(vr, mu, mu))
            INC("vector", "sDV",
                lambda e, vr=vr, ms=ms: e.tensor_sub(vr, ms, vr))
            vDV_var = FDV()
            W("scalar", "sDV", vDV_var)
            INC("scalar", "sAC", lambda e, vr=vr: e.activation(
                out=vr, in_=vr, func=AF.Ln, bias=cst.ap()[:, 0:1]))
            INC("scalar", "sAC", lambda e, rs=rs, vr=vr: e.activation(
                out=rs, in_=vr, func=AF.Exp, scale=-0.5))
            vAC_rs = FAC()
            return mu, rs, vAC_rs, vDV_cp

        def rstd_chain(L, c_in, c_out, epscol):
            W("scalar", "sDV", G["lnch"])
            INC("scalar", "sAC", lambda e, a=L[:, c_in:c_in + 1]:
                e.activation(out=a, in_=a, func=AF.Ln,
                             bias=cst.ap()[:, epscol:epscol + 1]))
            INC("scalar", "sAC",
                lambda e, o=L[:, c_out:c_out + 1],
                a=L[:, c_in:c_in + 1]: e.activation(out=o, in_=a,
                                                    func=AF.Exp, scale=-0.5))
            return FAC()

        G["lnch"] = 0

        def attn_geom(aa):
            nblk = min(aa + 1, 5)
            w = 128 * nblk
            kt0 = max(0, aa - 4)
            wrapped = aa >= KR and (aa % KR) < 4
            kcol0 = (kt0 % KR) * 128 if not wrapped else None
            return nblk, w, kt0, kcol0

        def stage_scores(aa, h):
            u = aa * H + h
            nblk, w, kt0, kcol0 = attn_geom(aa)
            scb = big0 if u % 2 == 0 else scp
            rows0 = 64 * (h % 2)
            ch = h // 2
            lhs = qfm[aa % 2].ap()[rows0:rows0 + 64, ch * 128:(ch + 1) * 128]
            kb = kfm.ap()[rows0:rows0 + 64,
                          ch * (KR + KMIR) * 128:(ch + 1) * (KR + KMIR) * 128]
            W("tensor", "sAC", last_exp.get(u - 2, 0))
            if kcol0 is not None:
                n1 = min(w, 512)
                r1 = kb[:, kcol0:kcol0 + n1]
                vPE_sc = INC("tensor", "sPE", lambda e, l=lhs, r=r1, n1=n1,
                             scb=scb: e.matmul(scb.ap()[:, 0:n1], l, r,
                                               start=True, stop=True))
                if w > 512:
                    r2 = kb[:, kcol0 + 512:kcol0 + w]
                    vPE_sc = INC("tensor", "sPE", lambda e, l=lhs, r=r2, w=w,
                                 scb=scb: e.matmul(scb.ap()[:, 512:w], l, r,
                                                   start=True, stop=True))
            else:
                vPE_sc = None
                for j in range(nblk):
                    kc = ((kt0 + j) % KR) * 128
                    rj = kb[:, kc:kc + 128]
                    fn = lambda e, l=lhs, r=rj, j=j, scb=scb: e.matmul(
                        scb.ap()[:, j * 128:(j + 1) * 128], l, r,
                        start=True, stop=True)
                    if j == nblk - 1:
                        vPE_sc = INC("tensor", "sPE", fn)
                    else:
                        OP("tensor", fn)
            cst = 256 if aa >= 4 else max(0, 128 * aa - 63)
            cw = 64 if aa >= 4 else min(64, 128 * aa + 1 - cst)
            W("vector", "sPE", vPE_sc)
            dstC = negC[aa % 2].ap()[:, h:h + 1]
            INC("vector", "sDV", lambda e, dstC=dstC, cst=cst, cw=cw, scb=scb:
                e.tensor_reduce(out=dstC, in_=scb.ap()[:, cst:cst + cw],
                                axis=AX.X, op=ALU.max, negate=True))
            vDV_mx = FDV()
            W("scalar", "sDV", vDV_mx)
            W("scalar", "sPE", expPEgate.get(u, 0))
            pb = probs[u % 4].ap()
            last_exp[u] = INC("scalar", "sAC", lambda e, pb=pb, w=w, dstC=dstC,
                              scb=scb: e.activation(
                out=pb[:, 0:w], in_=scb.ap()[:, 0:w],
                func=AF.Exp, bias=dstC, scale=1.0))
            moff = 0 if aa >= 4 else 512 - 128 * aa
            W("gpsimd", "sAC", last_exp[u])
            return INC("gpsimd", "sPL", lambda e, pb=pb, w=w, moff=moff:
                       e.tensor_mul(pb[:, 0:w], pb[:, 0:w],
                                    mask4.ap()[:, moff:moff + w]))

        def stage_transpose(aa, h, vPL_m):
            u = aa * H + h
            nblk, w, _, _ = attn_geom(aa)
            pb = probs[u % 4].ap()
            vPE = transpose_group(
                lambda c, pb=pb: pb[:, c * 128:(c + 1) * 128],
                nblk, [("sPL", vPL_m)], "pT")
            vPE_tr_u[u] = vPE
            dst = probsT[u % 4].ap()[:, 0:w]
            W("vector", "sPE", vPE_av_u.get(u - 4, 0))
            vDV_pt_u[u] = ("sDV", pt_evac(dst, w, vPE))
            expPEgate[u + 4] = vPE

        def stage_av(aa, h):
            u = aa * H + h
            nblk, w, kt0, _ = attn_geom(aa)
            W("tensor", "sAC", last_avev.get(u - 1, 0))
            W("tensor", vDV_pt_u[u][0], vDV_pt_u[u][1])
            pT = probsT[u % 4].ap()
            vPE_av = None
            for j in range(nblk):
                kt = kt0 + j
                vb = (kt % KR) * H * 65 + h * 65
                fn = lambda e, pT=pT, j=j, vb=vb, last=(j == nblk - 1): e.matmul(
                    avp.ap()[:, 0:65], pT[:, j * 128:(j + 1) * 128],
                    vtm.ap()[:, vb:vb + 65], start=(j == 0), stop=last)
                if j == nblk - 1:
                    vPE_av = INC("tensor", "sPE", fn)
                else:
                    OP("tensor", fn)
            vPE_av_u[u] = vPE_av
            W("scalar", "sPE", vPE_av)
            if h == 0:
                W("scalar", "sDV", G.get("anorm", 0))
            dstA = araw.ap()[:, h * 65:(h + 1) * 65]
            last_avev[u] = INC("scalar", "sAC", lambda e, dstA=dstA:
                               e.activation(out=dstA, in_=avp.ap()[:, 0:65],
                                            func=AF.Copy))

        def attn_half(aa, half):
            if half == 0:
                W("tensor", "sDV", vDV_qfm[aa])
                W("tensor", "sDV", vDV_kfm[aa])
                W("tensor", "sDV", vDV_vev[aa])
                G["attn_pend"] = []
            pend = G["attn_pend"]
            for h in range(8 * half, 8 * half + 8):
                m = stage_scores(aa, h)
                pend.append((h, m))
                if len(pend) >= 3:
                    hh, mm = pend.pop(0)
                    stage_transpose(aa, hh, mm)
                    if hh >= 2:
                        stage_av(aa, hh - 2)
                drain_chain(1)
            if half == 1:
                for hh, mm in pend:
                    stage_transpose(aa, hh, mm)
                for hh in range(H - 4, H):
                    stage_av(aa, hh)
                W("vector", "sAC", last_avev[aa * H + H - 1])
                a3 = araw.ap().rearrange("p (h c) -> p h c", c=65)
                INC("vector", "sDV", lambda e, a3=a3: e.reciprocal(
                    out=rcp[aa % 2].ap()[:, 0:16], in_=a3[:, :, 64:65]))
                rb_ = rcp[aa % 2].ap().rearrange(
                    "p (h u) -> p h u", u=1).broadcast_to([128, 16, 64])
                vno = INC("vector", "sDV", lambda e, a3=a3, rb_=rb_, aa=aa:
                          e.tensor_tensor(
                    out=attn[aa % 2].ap().rearrange("p (h c) -> p h c", c=64),
                    in0=a3[:, :, 0:64], in1=rb_, op=ALU.mult))
                G["anorm"] = vno
                vAC_attn_done[aa] = vno

        pending_chain = []

        def drain_chain(n=None):
            k = len(pending_chain) if n is None else min(n, len(pending_chain))
            for _ in range(k):
                pending_chain.pop(0)()

        def ln_stats_thunks(src_sb, L, epscol, pre_waits, x1b_gates):
            def t1():
                for eng, sem, val in pre_waits:
                    W(eng, sem, val())
                W("scalar", "sDV", G["sqf"])
                INC("scalar", "sAC", lambda e: e.activation(
                    out=ry.ap(), in_=src_sb.ap(), func=AF.Square,
                    accum_out=L[:, 1:2]))
                G["rysq"] = FAC()
                for fn in x1b_gates:
                    fn()
                INC("vector", "sDV", lambda e: e.tensor_reduce(
                    out=L[:, 0:1], in_=src_sb.ap(), axis=AX.X, op=ALU.add))

            def t2():
                W("vector", "sAC", G["rysq"])
                OP("vector", lambda e: e.tensor_single_scalar(
                    out=L[:, 2:3], in_=L[:, 0:1], scalar=1.0 / E, op=ALU.mult))
                OP("vector", lambda e: e.tensor_single_scalar(
                    out=L[:, 3:4], in_=L[:, 1:2], scalar=1.0 / E, op=ALU.mult))
                OP("vector", lambda e: e.tensor_mul(
                    L[:, 4:5], L[:, 2:3], L[:, 2:3]))
                INC("vector", "sDV", lambda e: e.tensor_sub(
                    L[:, 4:5], L[:, 3:4], L[:, 4:5]))
                G["lnch"] = FDV()
                G["rsv"] = rstd_chain(L, 4, 5, epscol)
            return [t1, t2]

        def make_pc_chain(p):
            L = lar[p % 2].ap()
            src = hb[p % 2]
            th = ln_stats_thunks(
                src, L, 0,
                pre_waits=[("scalar", "sDV", lambda p=p: vDV_h0_d[p])],
                x1b_gates=[
                    lambda p=p: W("vector", "sDV", vPL_h1_d.get(p - 2, 0)),
                    lambda p=p: W("vector", "sO", sO_d.get(p - 2, 0))])

            def t3():
                W("vector", "sAC", G["rsv"])
                vAC_x1[p] = INC("vector", "sDV", lambda e: e.tensor_scalar(
                    out=x1b[p % 2].ap(), in0=src.ap(), scalar1=L[:, 2:3],
                    scalar2=L[:, 5:6], op0=ALU.subtract, op1=ALU.mult))

            def t4():
                W("scalar", "sDV", vAC_x1[p])
                W("scalar", "sDV", G["sqf"])
                INC("scalar", "sAC", lambda e: e.activation(
                    out=ry.ap(), in_=x1b[p % 2].ap(), func=AF.Square,
                    accum_out=L[:, 12:13]))
                G["rysq"] = FAC()
                W("vector", "sAC", G["rysq"])
                OP("vector", lambda e: e.tensor_reduce(
                    out=L[:, 7:8], in_=x1b[p % 2].ap(), axis=AX.X, op=ALU.max,
                    apply_absolute_value=True))
                INC("vector", "sDV", lambda e: e.tensor_single_scalar(
                    out=L[:, 12:13], in_=L[:, 12:13], scalar=1.0 / E,
                    op=ALU.mult))
                G["lnch"] = FDV()
                G["rsv"] = rstd_chain(L, 12, 13, 1)

            def t5():
                W("vector", "sAC", G["rsv"])
                OP("vector", lambda e: e.tensor_mul(
                    L[:, 8:9], L[:, 7:8], L[:, 13:14]))
                OP("vector", lambda e: e.tensor_single_scalar(
                    out=L[:, 8:9], in_=L[:, 8:9], scalar=1e-5, op=ALU.max))
                OP("vector", lambda e: e.reciprocal(
                    out=L[:, 9:10], in_=L[:, 8:9]))
                OP("vector", lambda e: e.tensor_mul(
                    L[:, 10:11], L[:, 13:14], L[:, 9:10]))
                OP("vector", lambda e: e.tensor_single_scalar(
                    out=L[:, 10:11], in_=L[:, 10:11], scalar=127.0,
                    op=ALU.mult))
                W("vector", "sDV", vDV_wsr)
                OP("vector", lambda e: e.tensor_mul(
                    L[:, 11:12], L[:, 8:9], wsr.ap()))
                INC("vector", "sDV", lambda e: e.tensor_single_scalar(
                    out=L[:, 11:12], in_=L[:, 11:12], scalar=1.0 / 127,
                    op=ALU.mult))
                vDV_fsc[p] = FDV()

            def t6():
                OP("vector", lambda e: e.tensor_scalar(
                    out=ry.ap(), in0=x1b[p % 2].ap(), scalar1=L[:, 10:11],
                    scalar2=256.5, op0=ALU.mult, op1=ALU.add))
                OP("vector", lambda e: e.tensor_copy(
                    out=rci.ap(), in_=ry.ap()))
                G["sqf"] = INC("vector", "sDV", lambda e: e.tensor_tensor(
                    out=ry.ap(), in0=ry.ap(), in1=rci.ap(), op=ALU.subtract))

            def t7():
                OP("vector", lambda e: e.tensor_single_scalar(
                    out=re1.ap(), in_=ry.ap(), scalar=1.0, op=ALU.is_ge))
                OP("vector", lambda e: e.tensor_single_scalar(
                    out=re0.ap(), in_=ry.ap(), scalar=0.0, op=ALU.is_ge))
                OP("vector", lambda e: e.scalar_tensor_tensor(
                    out=ry.ap(), in0=rci.ap(), scalar=257.0, in1=re1.ap(),
                    op0=ALU.subtract, op1=ALU.add))
                G["sqf"] = vDV_q8[p] = INC(
                    "vector", "sDV", lambda e: e.tensor_tensor(
                        out=q8tm[p % 2].ap(), in0=ry.ap(), in1=re0.ap(),
                        op=ALU.add))
            return th + [t3, t4, t5, t6, t7]

        def make_fc_chain(f):
            L2 = lar[f % 2].ap()

            def ta():
                W("vector", "sAC", vAC_y0_d[f])
                OP("vector", lambda e: e.tensor_add(
                    hb[f % 2].ap(), hb[f % 2].ap(), fbrep.ap()))
                vPL_h1_d[f] = INC("vector", "sDV", lambda e: e.tensor_add(
                    hb[f % 2].ap(), hb[f % 2].ap(), x1b[f % 2].ap()))
            th = ln_stats_thunks(
                hb[f % 2], L2, 0,
                pre_waits=[("scalar", "sDV", lambda f=f: vPL_h1_d[f])],
                x1b_gates=[])

            def tb():
                W("vector", "sAC", G["rsv"])
                vAC_o = INC("vector", "sDV", lambda e: e.tensor_scalar(
                    out=x1b[f % 2].ap(), in0=hb[f % 2].ap(),
                    scalar1=L2[:, 2:3], scalar2=L2[:, 5:6],
                    op0=ALU.subtract, op1=ALU.mult))
                vAC_out_d[f] = vAC_o
                W("sync", "sDV", vAC_o)
                sO_d[f] = INC("sync", "sO", lambda e: e.dma_start(
                    out=d_out[f * 128:(f + 1) * 128, :],
                    in_=x1b[f % 2].ap()), 16)
            return [ta] + th + [tb]

        _dbg_rounds = int(os.environ.get("ATH_DEBUG_ROUNDS", NR))
        _skip = set(os.environ.get("ATH_SKIP", "").split(","))
        for r in range(_dbg_rounds):
            t, a, p, f = r, r - 2, r - 3, r - 4
            if "qkv" in _skip:
                t = NT
            if "attn" in _skip:
                a = -1
            if "proj" in _skip:
                p = -1
            if "ffn" in _skip:
                f = -1

            pc = r - 4 if "proj" not in _skip else -1
            fc = r - 5 if "ffn" not in _skip else -1
            if 0 <= pc < NT:
                pending_chain.extend(make_pc_chain(pc))
            if 0 <= fc < NT:
                pending_chain.extend(make_fc_chain(fc))

            if t == 0:
                x_transposes(0)
            if 1 <= t + 1 < NT:
                tt = t + 1
                W("gpsimd", "sPE", vPE_xtr_d[tt - 1])
                vG_x[tt] = INC("gpsimd", "sX", lambda e, tt=tt: e.dma_start(
                    out=xbf[0].ap(),
                    in_=d["x"][tt * 128:(tt + 1) * 128, :]), 16)
                x_transposes(tt)

            if t < NT:
                W("tensor", "sWQ", vG_wqkv)
                W("tensor", "sRW", vG_rows)
                W("tensor", "sDV", vDV_xfm[t])
                W("tensor", G["big1k"], G["big1"])
                W("tensor", "sDV", vDV_vones)
                vPE_qkv_v[t] = qkv_wave(t, 4)
                W("vector", "sPE", vPE_qkv_v[t])
                if t >= KR:
                    W("vector", "sDV", vAC_attn_done[t - (KR - 4)])
                slot = t % KR
                vv = None
                for h8 in range(2):
                    src_ = big1.ap()[:, h8 * 512:(h8 + 1) * 512].rearrange(
                        "p (h c) -> p h c", c=64)
                    base = slot * H * 65 + h8 * 8 * 65
                    dst = vtm.ap()[:, base:base + 8 * 65].rearrange(
                        "p (h c) -> p h c", c=65)[:, :, 0:64]
                    fn = lambda e, s=src_, dd=dst: e.tensor_copy(out=dd, in_=s)
                    if h8 == 1:
                        vv = INC("vector", "sDV", fn)
                    else:
                        OP("vector", fn)
                vDV_vev[t] = vv
                G["big1"], G["big1k"] = vv, "sDV"

            if t + 1 < NT and t >= 0:
                pass

            if t < NT:
                W("tensor", G["big1k"], G["big1"])
                vPE_qkv_q[t] = qkv_wave(t, 0)
                mu, rs, vAC_rsq, vDV_cpq = seg_stats(t, 0, big1.ap(),
                                                     vPE_qkv_q[t], 1e-5)
                G["big1"], G["big1k"] = vDV_cpq, "sDV"
            if 0 <= t - 1 < NT:
                tm1 = t - 1
                vPE_qtr = transpose_group(
                    lambda c, tm1=tm1: n_q.ap()[:, c * 128:(c + 1) * 128],
                    NCH, [("sDV", vDV_nq[tm1])], "q")
                vPE_qtr_d[tm1] = vPE_qtr
                vDV_qfm[tm1] = pt_evac(qfm[tm1 % 2].ap(), 1024, vPE_qtr)
            if 0 <= a < NT:
                attn_half(a, 0)

            if t < NT:
                W("vector", "sAC", vAC_rsq)
                W("vector", "sPE", vPE_qtr_d.get(t - 1, 0))
                zz3 = zzb.ap().rearrange("p (s u) -> p s u", u=64)
                nq3 = n_q.ap().rearrange("p (s u) -> p s u", u=64)
                mu_b = mu.rearrange("p (s u) -> p s u", u=1).broadcast_to(
                    [128, 16, 64])
                rs_b = rs.rearrange("p (s u) -> p s u", u=1).broadcast_to(
                    [128, 16, 64])
                OP("vector", lambda e, mu_b=mu_b, zz3=zz3, nq3=nq3:
                   e.tensor_tensor(out=nq3, in0=zz3, in1=mu_b,
                                   op=ALU.subtract))
                OP("vector", lambda e, rs_b=rs_b, nq3=nq3:
                   e.tensor_tensor(out=nq3, in0=nq3, in1=rs_b, op=ALU.mult))
                W("vector", "sDV", vDV_reps)
                OP("vector", lambda e: e.tensor_mul(
                    n_q.ap(), n_q.ap(), reps["gq"].ap()))
                vDV_nq[t] = INC("vector", "sDV", lambda e: e.tensor_add(
                    n_q.ap(), n_q.ap(), reps["bq"].ap()))

            if t < NT:
                W("tensor", G["big1k"], G["big1"])
                vPE_qkv_k[t] = qkv_wave(t, 2)
                mu2, rs2, vAC_rsk, vDV_cpk = seg_stats(t, 1, big1.ap(),
                                                       vPE_qkv_k[t], 1e-5)
                G["big1"], G["big1k"] = vDV_cpk, "sDV"
            if 0 <= t - 1 < NT:
                tm1 = t - 1
                vPE_ktr = transpose_group(
                    lambda c: n_k.ap()[:, c * 128:(c + 1) * 128], NCH,
                    [("sDV", vDV_nk[tm1])], "k")
                vPE_ktr_d[tm1] = vPE_ktr
                slot = tm1 % KR
                W("vector", "sPE", vPE_ktr)
                dst = kfm.ap().rearrange(
                    "p (c w_) -> p c w_", w_=(KR + KMIR) * 128)[
                    :, :, slot * 128:(slot + 1) * 128]
                vlast = INC("vector", "sDV", lambda e, dst=dst: e.tensor_copy(
                    out=dst, in_=ptb.rearrange("p (c w_) -> p c w_", w_=128)))
                vDV_kfm[tm1] = vlast
                G["ptev"] = ("sDV", vlast)
            if 0 <= a < NT:
                attn_half(a, 1)

            if t < NT:
                W("vector", "sAC", vAC_rsk)
                W("vector", "sPE", vPE_ktr_d.get(t - 1, 0))
                zz3 = zzb.ap().rearrange("p (s u) -> p s u", u=64)
                nk3 = n_k.ap().rearrange("p (s u) -> p s u", u=64)
                mu2_b = mu2.rearrange("p (s u) -> p s u", u=1).broadcast_to(
                    [128, 16, 64])
                rs2_b = rs2.rearrange("p (s u) -> p s u", u=1).broadcast_to(
                    [128, 16, 64])
                OP("vector", lambda e, mu2_b=mu2_b, zz3=zz3, nk3=nk3:
                   e.tensor_tensor(out=nk3, in0=zz3, in1=mu2_b,
                                   op=ALU.subtract))
                OP("vector", lambda e, rs2_b=rs2_b, nk3=nk3:
                   e.tensor_tensor(out=nk3, in0=nk3, in1=rs2_b, op=ALU.mult))
                OP("vector", lambda e: e.tensor_mul(
                    n_k.ap(), n_k.ap(), reps["gk"].ap()))
                vDV_nk[t] = INC("vector", "sDV", lambda e: e.tensor_add(
                    n_k.ap(), n_k.ap(), reps["bk"].ap()))

            drain_chain()

            if 0 <= p < NT:
                vPE_atr = transpose_group(
                    lambda c, p=p: attn[p % 2].ap()[:, c * 128:(c + 1) * 128],
                    NCH, [("sDV", vAC_attn_done[p])], "a")
                vDV_afm = pt_evac(afm[p % 2].ap(), 1024, vPE_atr)
                W("tensor", G["big1k"], G["big1"])
                W("tensor", "sWO", vG_wout)
                W("tensor", "sDV", vDV_afm)
                af = afm[p % 2].ap()
                vPE_proj = None
                for n in range(2):
                    for kk in range(NCH):
                        l = af[:, kk * 128:(kk + 1) * 128]
                        rr = wout.ap()[:, kk * E + n * 512:kk * E + (n + 1) * 512]
                        OP("tensor", lambda e, l=l, rr=rr, n=n, kk=kk: e.matmul(
                            big1.ap()[:, n * 512:(n + 1) * 512], l, rr,
                            start=(kk == 0), stop=False))
                    rb = boutr.ap()[0:1, n * 512:(n + 1) * 512]
                    vPE_proj = INC("tensor", "sPE", lambda e, rb=rb, n=n: e.matmul(
                        big1.ap()[:, n * 512:(n + 1) * 512],
                        onesb.ap()[0:1, 0:128], rb, start=False, stop=True))
                W("vector", "sPE", vPE_proj)
                W("vector", "sXR%d" % (p % 2), vS_xres[p])
                W("vector", "sDV", vAC_out_d.get(p - 2, 0))
                vDV_h0 = INC("vector", "sDV", lambda e, p=p: e.tensor_add(
                    hb[p % 2].ap(), big1.ap(), xres[p % 2].ap()))
                vDV_h0_d[p] = vDV_h0
                G["big1"], G["big1k"] = vDV_h0, "sDV"

            drain_chain()

            if 0 <= f < NT:
                vPE_q8t = transpose_group(
                    lambda c, f=f: q8tm[f % 2].ap()[:, c * 128:(c + 1) * 128],
                    NCH, [("sDV", vDV_q8[f])], "q8")
                vDV_q8f = pt_evac(q8fm[f % 2].ap(), 1024, vPE_q8t)
                W("tensor", G["big1k"], G["big1"])
                W("tensor", "sDV", vDV_q8f)
                qf = q8fm[f % 2].ap()
                vPE_ffn = None
                for n in range(2):
                    for kk in range(NCH):
                        l = qf[:, kk * 128:(kk + 1) * 128]
                        rr = wfT.ap()[:, kk * E + n * 512:kk * E + (n + 1) * 512]
                        fn = lambda e, l=l, rr=rr, n=n, kk=kk: e.matmul(
                            big1.ap()[:, n * 512:(n + 1) * 512], l, rr,
                            start=(kk == 0), stop=(kk == NCH - 1))
                        if n == 1 and kk == NCH - 1:
                            vPE_ffn = INC("tensor", "sPE", fn)
                        else:
                            OP("tensor", fn)
                L2 = lar[f % 2].ap()
                W("scalar", "sPE", vPE_ffn)
                W("scalar", "sDV", vDV_fsc[f])
                W("scalar", "sDV", vAC_x1[f])
                vAC_y0 = INC("scalar", "sAC", lambda e, f=f, L2=L2:
                             e.activation(out=hb[f % 2].ap(), in_=big1.ap(),
                                          func=AF.Copy,
                                          scale=L2[:, 11:12]))
                vAC_y0_d[f] = vAC_y0
                G["big1"], G["big1k"] = vAC_y0, "sAC"

            tp = p + 1
            if 0 <= tp < NT:
                if tp >= 2:
                    W("sync", "sDV", vAC_x1[tp - 2])
                vS_xres[tp] = INC("sync", "sXR%d" % (tp % 2),
                                  lambda e, tp=tp: e.dma_start(
                    out=xres[tp % 2].ap(),
                    in_=d["x"][tp * 128:(tp + 1) * 128, :]), 16)

        if os.environ.get("ATH_DUMP"):
            dumps = [("d_nq", n_q), ("d_nk", n_k), ("d_xfm0", xfm[0]),
                     ("d_qfm0", qfm[0]), ("d_kfm", kfm), ("d_vtm", vtm),
                     ("d_attn0", attn[0]), ("d_attn1", attn[1]),
                     ("d_probs0", probs[0]), ("d_probsT0", probsT[0]),
                     ("d_x1b0", x1b[0]), ("d_hb0", hb[0]),
                     ("d_q8tm0", q8tm[0]), ("d_wfT", wfT),
                     ("d_negC0", negC[0]), ("d_rcp0", rcp[0]),
                     ("d_ar0", ar[0]), ("d_lar0", lar[0]),
                     ("d_fbrep", fbrep), ("d_gqrep", reps["gq"]),
                     ("d_wqkv", wqkv)]
            for sname in ["sPE", "sDV", "sAC", "sPL"]:
                W("sync", sname, S_.cnt[sname])
            for dn, tens in dumps:
                shape = list(tens.ap().shape)
                dt_ = tens.ap().dtype
                dram = nc.dram_tensor(dn, shape, dt_, kind="ExternalOutput").ap()
                INC("sync", "sO", lambda e, a=dram, b=tens: e.dma_start(
                    out=a, in_=b.ap()), 16)
        W("sync", "sO", S_.cnt["sO"])

        with nc.Block() as block:
            @block.sync
            def _(e):
                for fn in S_.plan["sync"]:
                    fn(e)

            @block.tensor
            def _(e):
                for fn in S_.plan["tensor"]:
                    fn(e)

            @block.vector
            def _(e):
                for fn in S_.plan["vector"]:
                    fn(e)

            @block.scalar
            def _(e):
                for fn in S_.plan["scalar"]:
                    fn(e)

            @block.gpsimd
            def _(e):
                for fn in S_.plan["gpsimd"]:
                    fn(e)

    return nc


def kernel(L0_attn_x, L0_attn_W_qkv, L0_attn_b_qkv, L0_attn_q_gamma,
           L0_attn_q_beta, L0_attn_k_gamma, L0_attn_k_beta, L0_attn_W_out,
           L0_attn_b_out, L0_ffn_w_int8, L0_ffn_weight_scale, L0_ffn_bias):
    nc = build_nc()
    x = np.ascontiguousarray(np.asarray(L0_attn_x, dtype=np.float32))
    com = {
        "wqkv": np.ascontiguousarray(np.asarray(L0_attn_W_qkv, np.float32)),
        "bqkv": np.asarray(L0_attn_b_qkv, np.float32).reshape(1, -1),
        "gq": np.asarray(L0_attn_q_gamma, np.float32).reshape(1, -1),
        "bq": np.asarray(L0_attn_q_beta, np.float32).reshape(1, -1),
        "gk": np.asarray(L0_attn_k_gamma, np.float32).reshape(1, -1),
        "bk": np.asarray(L0_attn_k_beta, np.float32).reshape(1, -1),
        "wout": np.ascontiguousarray(np.asarray(L0_attn_W_out, np.float32)),
        "bout": np.asarray(L0_attn_b_out, np.float32).reshape(1, -1),
        "w8": np.ascontiguousarray(np.asarray(L0_ffn_w_int8, np.int32)),
        "ws": np.asarray(L0_ffn_weight_scale, np.float32).reshape(1, 1),
        "fb": np.asarray(L0_ffn_bias, np.float32).reshape(1, -1),
        "mask4": _mask4_np(),
        "ident": np.eye(128, dtype=np.float32).astype(BF16NP),
    }
    in_maps = [dict(com, x=np.ascontiguousarray(x[b])) for b in range(8)]
    res = run_bass_kernel_spmd(nc, in_maps, core_ids=list(range(8)))
    return np.stack([np.asarray(r["out"], dtype=np.float32)
                     for r in res.results], axis=0)
